# version 25 (final)
# speedup vs baseline: 1.0967x; 1.0445x over previous
import numpy as np
import concourse.bass as bass
import concourse.mybir as mybir
from concourse.bass_utils import run_bass_kernel_spmd

F32 = mybir.dt.float32
F32R = mybir.dt.float32r
BF16 = mybir.dt.bfloat16
I32 = mybir.dt.int32
AF = mybir.ActivationFunctionType
OP = mybir.AluOpType
AX = mybir.AxisListType.X

D = 1024
DIN = 2048
NH = 32
HD = 64
NSTATE = 128
CONVD = 2304
NCC = CONVD // 128
DIP = 4384
HMLP = 4096
Q = 128

ENGS = ("sync", "scalar", "vector", "tensor", "gpsimd")


class Builder:
    def __init__(self, T, slab, debug_outs=()):
        assert T % slab == 0 and slab % Q == 0
        self.T, self.slab = T, slab
        self.nslab = T // slab
        self.ncs = slab // Q
        self.nchunk = T // Q
        nc = self.nc = bass.Bass("TRN2", target_bir_lowering=False)
        self.prog = {e: [] for e in ENGS}
        self.sems = {}
        self.cnt = {}
        for s in ("mm", "dv", "ac", "cc"):
            self.sems[s] = nc.alloc_semaphore(s)
            self.cnt[s] = 0
        self._wmax = {}
        self.qpool = {}
        self.qidx = {}
        for q, npool in (("sld", 4), ("gld", 4), ("ost", 2)):
            names = []
            for i in range(npool):
                n = f"{q}{i}"
                self.sems[n] = nc.alloc_semaphore(n)
                self.cnt[n] = 0
                names.append(n)
            self.qpool[q] = names
            self.qidx[q] = 0
        self.debug_outs = list(debug_outs)

    def emit(self, eng, fn):
        self.prog[eng].append(fn)

    def wait(self, eng, s, v=None):
        v = self.cnt[s] if v is None else v
        if v > 0:
            self.emit(eng, lambda E, _s=self.sems[s], _v=v: E.wait_ge(_s, _v))

    _AUTOW = {"vector": ("dv",), "scalar": ("ac",)}

    def op(self, eng, fn, s=None, n=1):
        for sw in self._AUTOW.get(eng, ()):
            if self.cnt[sw] > 0:
                self.emit(eng, lambda E, _s=self.sems[sw], _v=self.cnt[sw]: E.wait_ge(_s, _v))
        if s is not None:
            self.cnt[s] += n
            self.emit(eng, lambda E, _f=fn, _s=self.sems[s], _n=n: _f(E).then_inc(_s, _n))
            return self.cnt[s]
        self.emit(eng, lambda E, _f=fn: _f(E))
        return None

    def dma(self, eng, outap, inap, s=None, slow=False):
        q = "gld" if (s is None and eng == "gpsimd") else ("sld" if s is None else s)
        pool = self.qpool[q]
        name = pool[self.qidx[q] % len(pool)]
        self.qidx[q] += 1
        self.wait(eng, name, self.cnt[name])
        if slow:
            v = self.op(eng, lambda E, _o=outap, _i=inap: E.dma_start(
                out=_o, in_=_i, allow_slow_non_contiguous=True), name, 16)
        else:
            v = self.op(eng, lambda E, _o=outap, _i=inap: E.dma_start(out=_o, in_=_i), name, 16)
        return (name, v)

    def wtok(self, eng, tok):
        if tok is None:
            return
        self.wait(eng, tok[0], tok[1])

    def wq(self, eng, q):
        for name in self.qpool[q]:
            self.wait(eng, name, self.cnt[name])

    def tt(self, out, a, b, o=OP.mult, s="dv", eng="vector"):
        return self.op(eng, lambda E: E.tensor_tensor(out=out, in0=a, in1=b, op=o), s)

    def ts(self, out, a, s1, s2=None, o1=OP.mult, o2=None, accum=None, eng="vector"):
        if o2 is None:
            return self.op(eng, lambda E: E.tensor_scalar(out=out, in0=a, scalar1=s1,
                                                          scalar2=None, op0=o1, accum_out=accum), "dv")
        return self.op(eng, lambda E: E.tensor_scalar(out=out, in0=a, scalar1=s1, scalar2=s2,
                                                      op0=o1, op1=o2, accum_out=accum), "dv")

    def stt(self, out, a, sc, b, o1, o2, accum=None):
        return self.op("vector", lambda E: E.scalar_tensor_tensor(
            out=out, in0=a, scalar=sc, in1=b, op0=o1, op1=o2, accum_out=accum), "dv")

    def cp(self, out, a, eng="vector"):
        return self.op(eng, lambda E: E.tensor_copy(out, a), "dv")

    def act(self, out, a, f, bias=None, scale=None):
        kw = {}
        if bias is not None:
            kw["bias"] = bias
        if scale is not None:
            kw["scale"] = scale
        return self.op("scalar", lambda E: E.activation(out=out, in_=a, func=f, **kw), "ac")

    def mm(self, psum, lhsT, rhs, start, stop):
        return self.op("tensor", lambda E: E.matmul(psum, lhsT, rhs, start=start, stop=stop), "mm")

    def tr(self, psum, data, ident):
        return self.op("tensor", lambda E: E.transpose(psum, data, ident), "mm")

    def red(self, out, a, o=OP.max, absv=True):
        return self.op("vector", lambda E: E.tensor_reduce(
            out=out, in_=a, axis=AX, op=o, apply_absolute_value=absv), "dv")

    def barrier(self, engs=("tensor", "vector", "scalar"), sems_=("mm", "dv", "ac")):
        for e in engs:
            for s in sems_:
                self.wait(e, s)

    @staticmethod
    def bco(ap2d, reps):
        P, K = ap2d.shape
        return ap2d.unsqueeze(2).broadcast_to((P, K, reps))

    @staticmethod
    def bci(ap2d, reps):
        P, K = ap2d.shape
        return ap2d.unsqueeze(1).broadcast_to((P, reps, K))

    @staticmethod
    def r3(ap2d, k, t):
        return ap2d.rearrange("p (k t) -> p k t", k=k)

    def build(self):
        nc, T, slab = self.nc, self.T, self.slab
        nslab, ncs, nchunk = self.nslab, self.ncs, self.nchunk

        def din(name, shape, dt=F32):
            return nc.dram_tensor(name, shape, dt, kind="ExternalInput").ap()

        xin = din("xin", [128 + T, D])
        win_t = din("win_t", [D, DIP])
        wout_t = din("wout_t", [DIN, D])
        w1_t = din("w1_t", [D, HMLP])
        w2_t = din("w2_t", [HMLP, D])
        fw1_t = din("fw1_t", [D, HMLP])
        fw2_t = din("fw2_t", [HMLP, D])
        convw = din("convw", [128, NCC * 4])
        convb = din("convb", [128, NCC])
        norm1w = din("norm1w", [128, 8])
        norm2w = din("norm2w", [128, 8])
        mnormw = din("mnormw", [128, 16])
        dtb = din("dtb", [NH, 1])
        alog = din("alog", [NH, 1])
        alog_row = din("alog_row", [1, NH])
        dtile = din("dtile", [128, DIN])
        b1 = din("b1", [128, 32])
        b2 = din("b2", [128, 8])
        fb1 = din("fb1", [128, 32])
        fb2 = din("fb2", [128, 8])
        eye_f = din("eye_f", [128, 128])
        triu_f = din("triu_f", [128, 128])
        ptl_f = din("ptl_f", [128, 128])
        ones_row = din("ones_row", [1, 128])
        gmask = din("gmask", [8, 1])
        umat_t = din("umat_t", [8, 8])

        out = nc.dram_tensor("out", [T, D], F32, kind="ExternalOutput").ap()

        def ddr(name, shape, dt):
            return nc.dram_tensor(name, shape, dt).ap()

        st_ypre = ddr("st_ypre", [nchunk, 128, DIN], BF16)
        st_zs = ddr("st_zs", [nslab, 128, 16 * slab], BF16)
        st_xnfm = ddr("st_xnfm", [nslab, 128, 8 * slab], F32)
        st_xpxn = ddr("st_xpxn", [T, D], BF16)
        st_xattn = ddr("st_xattn", [nslab, 128, 8 * slab], BF16)
        st_hin = ddr("st_hin", [nslab, 128, 8 * slab], BF16)
        st_attno = ddr("st_attno", [nslab, 128, 8 * slab], BF16)
        st_h = ddr("st_h", [nslab, 128, 32 * slab], BF16)
        sc_w1q = ddr("sc_w1q", [128, 8 * HMLP], BF16)
        sc_w2q = ddr("sc_w2q", [8, 128, 32 * 128], BF16)
        sc_w1f = ddr("sc_w1f", [128, 8 * HMLP], BF16)
        sc_w2f = ddr("sc_w2f", [8, 128, 32 * 128], BF16)
        CC1W = 2082
        cc1_in = ddr("cc1_in", [128, CC1W], F32)
        cc1_out = ddr("cc1_out", [8, 128, CC1W], F32)
        cc2_in = ddr("cc2_in", [1, 1], F32)
        cc2_out = ddr("cc2_out", [8, 1], F32)

        dbg_aps = {}
        for name in self.debug_outs:
            pass

        def sbuf(name, shape, dt):
            return nc.alloc_sbuf_tensor(name, shape, dt).ap()

        c_eye = sbuf("c_eye", [128, 128], F32)
        c_eyeb = sbuf("c_eyeb", [128, 128], BF16)
        c_triu = sbuf("c_triu", [128, 128], F32)
        c_triub = sbuf("c_triub", [128, 128], BF16)
        c_ptl = sbuf("c_ptl", [128, 128], F32R)
        c_onesr = sbuf("c_onesr", [1, 128], F32)
        c_onescb = sbuf("c_onescb", [128, 1], BF16)
        c_dtile = sbuf("c_dtile", [128, DIN], BF16)
        c_convw = sbuf("c_convw", [128, NCC * 4], F32)
        c_convb = sbuf("c_convb", [128, NCC], F32)
        c_n1w = sbuf("c_n1w", [128, 8], F32)
        c_n2w = sbuf("c_n2w", [128, 8], F32)
        c_mnw = sbuf("c_mnw", [128, 16], F32)
        c_dtb = sbuf("c_dtb", [NH, 1], F32)
        c_negA = sbuf("c_negA", [NH, 1], F32)
        c_negAt = sbuf("c_negAt", [128, NH], F32)
        c_b1 = sbuf("c_b1", [128, 32], F32)
        c_b2 = sbuf("c_b2", [128, 8], F32)
        c_fb1 = sbuf("c_fb1", [128, 32], F32)
        c_fb2 = sbuf("c_fb2", [128, 8], F32)
        c_gmask = sbuf("c_gmask", [8, 1], F32)
        c_umat = sbuf("c_umat", [8, 8], F32)
        t_a = sbuf("t_a", [NH, 1], F32)
        t_ar = sbuf("t_ar", [1, NH], F32)

        p_Csm = sbuf("p_Csm", [128, nchunk * 128], BF16)
        p_expl = sbuf("p_expl", [128, nchunk * NH], BF16)
        p_dab = sbuf("p_dab", [128, nchunk * NH], F32)

        p_Srun = sbuf("p_Srun", [128, DIN], F32)
        p_sumL = sbuf("p_sumL", [1, NH], F32)
        p_maxxn = sbuf("p_maxxn", [128, 1], F32)
        p_maxh = sbuf("p_maxh", [128, 1], F32)
        p_Sprop = sbuf("p_Sprop", [128, DIN], BF16)
        p_w8 = sbuf("p_w8", [8, NH], F32)
        p_w8r = sbuf("p_w8r", [1, 8 * NH], F32)
        p_wbj = sbuf("p_wbj", [128, NH], F32)
        p_sc = sbuf("p_sc", [1, 16], F32)
        p_rq1 = sbuf("p_rq1", [128, 1], F32)
        p_rq2 = sbuf("p_rq2", [128, 1], F32)
        p_fs1 = sbuf("p_fs1", [128, 1], F32)
        p_fs2 = sbuf("p_fs2", [128, 1], F32)
        p_b1s = sbuf("p_b1s", [128, 32], F32)
        p_fb2s = sbuf("p_fb2s", [128, 8], F32)
        p_stats = sbuf("p_stats", [128, 12], F32)
        p_mv = sbuf("p_mv", [128, 2], F32)
        p_rr = sbuf("p_rr", [128, 40], F32)

        AB_N = 61184
        AF_N = 11392
        ar_bf = sbuf("ar_bf", [128, AB_N], BF16)
        ar_f = sbuf("ar_f", [128, AF_N], F32)
        d2_t = sbuf("d2_t", [128, (NH // 2) * 128], F32R)

        class Alloc:
            def __init__(self):
                self.bf = 0
                self.f = 0

            def abf(self, n):
                a = self.bf
                self.bf += n
                assert self.bf <= AB_N, f"bf arena overflow {self.bf}"
                return ar_bf[:, a:a + n]

            def af(self, n):
                a = self.f
                self.f += n
                assert self.f <= AF_N, f"f32 arena overflow {self.f}"
                return ar_f[:, a:a + n]

        pb = [nc.alloc_psum_tensor(f"pb{i}", [128, 512], F32) for i in range(8)]
        pbb = [p.bitcast(BF16) for p in pb]

        emit, wait, op, dma = self.emit, self.wait, self.op, self.dma
        wtok, wq = self.wtok, self.wq
        tt, ts, stt, cp, act, mm, tr, red = (self.tt, self.ts, self.stt, self.cp,
                                             self.act, self.mm, self.tr, self.red)
        bco, bci, r3 = self.bco, self.bci, self.r3
        cnt, sems = self.cnt, self.sems

        tk = {}
        bank_u = {}
        dbg_slots = {}

        def bwait(eng, b):
            u = bank_u.get(b)
            if u is not None:
                wait(eng, u[0], u[1])

        def bset(b, semn):
            bank_u[b] = (semn, cnt[semn])

        def mkdbg(name, src_ap, shape=None, dt=None):
            if name not in self.debug_outs:
                return
            shape = list(src_ap.shape) if shape is None else shape
            dt = src_ap.dtype if dt is None else dt
            d = nc.dram_tensor("dbg_" + name, shape, dt, kind="ExternalOutput").ap()
            self.wait("gpsimd", "dv")
            self.wait("gpsimd", "ac")
            self.wait("gpsimd", "mm")
            dma("gpsimd", d, src_ap, "gld")

        A0 = Alloc()
        q_stage = A0.af(HMLP)
        q_t1 = A0.af(2048)
        q_ge = A0.af(2048)
        q_int = A0.af(2048).bitcast(I32)
        q_bf = A0.abf(HMLP)
        q_red = A0.af(2)
        q_stage2 = A0.abf(2 * HMLP).bitcast(F32)
        q_bf2 = A0.abf(HMLP)
        q_stg = [q_stage, q_stage2]
        q_bfp = [q_bf, q_bf2]

        dma("sync", c_eye[:, :], eye_f)
        dma("sync", c_triu[:, :], triu_f)
        dma("sync", c_onesr[:, :], ones_row)
        dma("sync", c_n1w[:, :], norm1w)
        dma("sync", c_n2w[:, :], norm2w)
        dma("sync", c_mnw[:, :], mnormw)
        dma("sync", c_dtb[:, :], dtb)
        dma("sync", c_b1[:, :], b1)
        dma("sync", c_b2[:, :], b2)
        dma("sync", c_fb1[:, :], fb1)
        dma("sync", c_fb2[:, :], fb2)
        dma("sync", c_gmask[:, :], gmask)
        dma("sync", c_umat[:, :], umat_t)
        dma("sync", c_convb[:, :], convb)
        dma("sync", t_a[:, :], alog)
        dma("sync", t_ar[:, :], alog_row)
        dma("gpsimd", c_eyeb[:, :], eye_f)
        dma("gpsimd", c_triub[:, :], triu_f)
        dma("gpsimd", c_ptl[:, :], ptl_f)
        dma("gpsimd", c_onescb[:, :], ones_row.rearrange("o p -> p o"))
        dma("gpsimd", c_dtile[:, :], dtile)
        dma("sync", c_convw[:, :], convw)

        wq("scalar", "sld")
        act(c_negA[:, :], t_a[:, :], AF.Exp)
        wait("vector", "ac")
        ts(c_negA[:, :], c_negA[:, :], -1.0)
        act(t_ar[:, :], t_ar[:, :], AF.Exp)
        wait("vector", "ac")
        ts(t_ar[:, :], t_ar[:, :], -1.0)
        wait("tensor", "dv")
        wq("tensor", "gld")
        mm(pb[0][:, 0:NH], c_onesr[0:1, :], t_ar[0:1, 0:NH], True, True)
        wait("vector", "mm")
        cp(c_negAt[:, :], pb[0][:, 0:NH])

        op("vector", lambda E: E.memset(p_Srun[:, :], 0.0), "dv")
        op("vector", lambda E: E.memset(p_maxxn[:, :], 0.0), "dv")
        op("vector", lambda E: E.memset(p_maxh[:, :], 0.0), "dv")
        op("vector", lambda E: E.memset(p_sumL[:, :], 0.0), "dv")

        def phase_barrier():
            for e in ENGS:
                for cs in ("mm", "dv", "ac"):
                    wait(e, cs)
                wq(e, "sld")
                wq(e, "gld")
                wq(e, "ost")

        def bcast_col(dst, src_row):
            n = src_row.shape[1]
            wait("tensor", "dv")
            bwait("tensor", 6)
            mm(pb[6][:, 0:n], c_onesr[0:1, :], src_row, True, True)
            mv = cnt["mm"]
            wait("vector", "mm", mv)
            cp(dst, pb[6][:, 0:n])
            bset(6, "dv")

        def quant_round(dst_bf, src, w, iscr, fscr):
            cp(iscr[:, 0:w], src)
            cp(fscr[:, 0:w], iscr[:, 0:w])
            tt(fscr[:, 0:w], src, fscr[:, 0:w], OP.subtract)
            ts(dst_bf, fscr[:, 0:w], -0.5, -1.0, OP.is_le, OP.mult)
            stt(dst_bf, fscr[:, 0:w], 0.5, dst_bf, OP.is_ge, OP.add)
            cp(fscr[:, 0:w], iscr[:, 0:w])
            tt(dst_bf, fscr[:, 0:w], dst_bf, OP.add)

        q_lc = {}
        q_st = {}

        def qload(i, width, wsrc, roff):
            b = q_stg[i % 2]
            if q_lc.get(i % 2) is None:
                wait("sync", "dv")
            else:
                wait("sync", "dv", q_lc[i % 2])
            v = dma("sync", b[:, 0:width], wsrc[roff * 128:(roff + 1) * 128, :])
            wtok("vector", v)
            return b

        def qstore_guard(i):
            if q_st.get(i % 2) is None:
                wq("vector", "gld")
            else:
                wtok("vector", q_st[i % 2])

        def wquant(wsrc, nchunks, width, col, scratch, dc_major=False):
            HW = 2048
            for c in range(nchunks):
                b = qload(c, width, wsrc, c)
                red(q_red[:, 0:1], b[:, 0:width])
                if c == 0:
                    cp(q_red[:, 1:2], q_red[:, 0:1])
                else:
                    tt(q_red[:, 1:2], q_red[:, 1:2], q_red[:, 0:1], OP.max)
                q_lc[c % 2] = cnt["dv"]
            wait("tensor", "dv")
            tr(pb[1][0:1, 0:128], q_red[:, 1:2], c_eye[:, :])
            wait("vector", "mm")
            red(p_sc[0:1, 15:16], pb[1][0:1, 0:128], OP.max, absv=False)
            ts(p_sc[0:1, col:col + 1], p_sc[0:1, 15:16], 1e-6, 1.0 / 127.0, OP.max, OP.mult)
            op("vector", lambda E: E.reciprocal(p_sc[0:1, col + 1:col + 2], p_sc[0:1, col:col + 1]), "dv")
            bcast_col(q_red[:, 0:1], p_sc[0:1, col + 1:col + 2])
            for c in range(nchunks):
                b = qload(c, width, wsrc, c)
                qb = q_bfp[c % 2]
                qstore_guard(c)
                for h0 in range(0, width, HW):
                    w2 = min(HW, width - h0)
                    ts(q_t1[:, 0:w2], b[:, h0:h0 + w2], q_red[:, 0:1])
                    quant_round(qb[:, h0:h0 + w2], q_t1[:, 0:w2], w2, q_int, q_ge)
                q_lc[c % 2] = cnt["dv"]
                dv = cnt["dv"]
                wait("gpsimd", "dv", dv)
                if dc_major:
                    q_st[c % 2] = dma("gpsimd",
                        scratch[:, :, c * 128:(c + 1) * 128].rearrange("d p c -> p d c"),
                        qb[:, 0:width].rearrange("p (d c) -> p d c", d=8), "gld")
                else:
                    q_st[c % 2] = dma("gpsimd", scratch[:, c * width:(c + 1) * width],
                                      qb[:, 0:width], "gld")

        wquant(fw1_t, 8, HMLP, 0, sc_w1q)
        wquant(fw2_t, 32, D, 2, sc_w2q, dc_major=True)
        for k in range(32):
            b = qload(k, D, w2_t, k)
            qb = q_bfp[k % 2]
            qstore_guard(k)
            cp(qb[:, 0:D], b[:, 0:D])
            q_lc[k % 2] = cnt["dv"]
            dv = cnt["dv"]
            wait("gpsimd", "dv", dv)
            q_st[k % 2] = dma("gpsimd",
                sc_w2f[:, :, k * 128:(k + 1) * 128].rearrange("d p c -> p d c"),
                qb[:, 0:D].rearrange("p (d c) -> p d c", d=8), "gld")
        for k in range(8):
            b = qload(k, HMLP, w1_t, k)
            qb = q_bfp[k % 2]
            qstore_guard(k)
            ts(qb[:, 0:HMLP], b[:, 0:HMLP], c_n2w[:, k:k + 1])
            q_lc[k % 2] = cnt["dv"]
            dv = cnt["dv"]
            wait("gpsimd", "dv", dv)
            q_st[k % 2] = dma("gpsimd", sc_w1f[:, k * HMLP:(k + 1) * HMLP],
                              qb[:, 0:HMLP], "gld")

        phase_barrier()
        A = Alloc()
        a_x = A.af(4 * D)
        a_xn = A.af(D)
        a_l = A.af(slab)
        a_dl = A.af(slab)
        a_dt = A.af(slab)
        a_ltm = A.af(4 * NH)
        a_deltm = A.af(4 * NH)
        a_lendb = A.af(4 * NH)
        a_bn1 = A.abf(D)
        a_bn1fm = A.abf(8 * slab)
        a_wv = A.abf(2 * 8 * 512)
        a_wdt = A.abf(8 * NH)
        a_xbc = A.abf(NCC * (slab + 3))
        a_xbco = A.abf(NCC * slab)
        a_zsf = A.abf(16 * slab)
        a_cu = A.abf(1024)
        a_ypre = A.abf(DIN)
        a_xhtm = A.abf(2 * DIN)
        a_xw = A.abf(2 * DIN)
        a_xw2 = A.abf(2 * DIN)
        a_dxh = a_xw2
        a_btm = A.abf(2 * 128)
        a_wp = A.abf((NH // 2) * 128)
        a_np = None
        a_gc = A.abf(2 * 128)
        a_srb = A.abf(DIN)
        a_dtmb = A.abf(4 * NH)
        a_wcol = A.abf(4 * NH)
        a_y1 = a_xw
        a_np = a_zsf
        a_lrow = A.af(4 * NH)
        a_xnf = A.af(8 * 128)
        assert A.bf <= AB_N and A.f <= AF_N

        def rsqrt_newton(dst, src, scr0, scr1):
            vd = cnt["dv"]
            wait("scalar", "dv", vd)
            act(scr0, src, AF.Sqrt)
            wait("vector", "ac")
            op("vector", lambda E: E.reciprocal(scr1, scr0), "dv")
            tt(scr0, scr1, scr1)
            tt(scr0, scr0, src)
            ts(scr0, scr0, -0.5, 1.5, OP.mult, OP.add)
            tt(dst, scr1, scr0)

        def subtile(row, i, buf, pre, tok=None, st=None):
            xap = a_x[:, buf * D:(buf + 1) * D]
            if tok is None:
                wait("sync", "dv")
                wait("sync", "ac")
                tok = dma("sync", xap, xin[row:row + 128, :])
            if st is None:
                wtok("vector", tok)
                stt(a_xn[:, :], xap, 1.0, xap, OP.mult, OP.bypass, accum=p_stats[:, 0:1])
                stt(a_xn[:, :], xap, 1.0, xap, OP.mult, OP.mult, accum=p_stats[:, 1:2])
                m = p_mv[:, 0:1]
                var = p_mv[:, 1:2]
                ts(m, p_stats[:, 0:1], 1.0 / D)
                ts(p_stats[:, 2:3], p_stats[:, 1:2], 1.0 / D)
                tt(p_stats[:, 3:4], m, m)
                tt(var, p_stats[:, 2:3], p_stats[:, 3:4], OP.subtract)
                vpe = p_rr[:, 0:1]
                rstd = p_rr[:, 1:2]
                r2 = p_rr[:, 2:3]
                sb_ = p_rr[:, 3:4]
                sc2 = p_rr[:, 4:5]
                sb2 = p_rr[:, 5:6]
                s0 = p_rr[:, 6:7]
                s1 = p_rr[:, 7:8]
                ts(vpe, var, 1e-5, o1=OP.add)
                rsqrt_newton(rstd, vpe, s0, s1)
                tt(r2, rstd, rstd)
                tt(r2, r2, var)
                ts(r2, r2, 1024.0, 1e-6, OP.mult, OP.add)
                rsqrt_newton(r2, r2, s0, s1)
                tt(sb_, m, rstd)
                ts(sb_, sb_, -1.0)
                tt(sc2, rstd, r2)
                tt(sb2, sb_, r2)
            else:
                rstd, sb_, sc2, sb2 = st
            vd = cnt["dv"]
            wait("scalar", "dv", vd)
            if not pre:
                act(a_xn[:, :], xap, AF.Identity, scale=rstd, bias=sb_)
            act(a_bn1[:, :], xap, AF.Identity, scale=sc2, bias=sb2)
            acd = cnt["ac"]
            tk[("xc", buf)] = ("ac", acd)
            if not pre:
                wait("vector", "ac", acd)
                red(p_rr[:, 8:9], a_xn[:, :])
                tt(p_maxxn[:, :], p_maxxn[:, :], p_rr[:, 8:9], OP.max)
            for half in range(2):
                wait("tensor", "ac", acd)
                wait("tensor", "dv")
                for j in range(4):
                    c = half * 4 + j
                    tr(pbb[j][:, 0:128], a_bn1[:, c * 128:(c + 1) * 128], c_eyeb[:, :])
                    if not pre:
                        tr(pb[4 + j][:, 0:128], a_xn[:, c * 128:(c + 1) * 128], c_eye[:, :])
                mv = cnt["mm"]
                wait("vector", "mm", mv)
                if half == 0 and i == 0:
                    wtok("vector", tk.get("bn1fm"))
                for j in range(4):
                    c = half * 4 + j
                    ts(a_bn1fm[:, c * slab + i * 128:c * slab + (i + 1) * 128],
                       pbb[j][:, 0:128], c_n1w[:, c:c + 1])
                    bset(j, "dv")
                if not pre:
                    if half == 0:
                        wtok("vector", tk.get("xnf"))
                    for j in range(4):
                        c = half * 4 + j
                        cp(a_xnf[:, c * 128:(c + 1) * 128], pb[4 + j][:, 0:128])
                        bset(4 + j, "dv")
            if not pre:
                s_idx = (row - 128) // slab
                dv = cnt["dv"]
                wait("gpsimd", "dv", dv)
                tk["xnf"] = dma("gpsimd",
                    st_xnfm[s_idx].rearrange("p (c t) -> p c t", c=8)[:, :, i * 128:(i + 1) * 128],
                    a_xnf.rearrange("p (c t) -> p c t", c=8), "gld")

        def inproj_waves(width, pre):
            waves = []
            if pre != "z":
                waves += [("x", w) for w in range(0, NCC, 4)]
            if pre == "z":
                waves += [("z", w) for w in range(0, 16, 4)]
            for wv_i, (kind, w0) in enumerate(waves):
                par = wv_i % 2
                bank0 = par * 4
                nchk = min(4, (16 if kind == "z" else NCC) - w0)
                colbase = (0 if kind == "z" else DIN) + w0 * 128
                ncols = nchk * 128
                buf = a_wv[:, par * 4096:par * 4096 + 8 * ncols]
                wtok("gpsimd", tk.get(("wv", par)))
                v = dma("gpsimd", buf.rearrange("p (k c) -> p k c", k=8),
                        win_t[:, colbase:colbase + ncols].rearrange("(k p) c -> p k c", p=128), "gld")
                last_wave = kind == "x" and w0 + 4 >= NCC and pre is False
                if last_wave:
                    vdt = dma("gpsimd", a_wdt[:, 0:8 * NH].rearrange("p (k c) -> p k c", k=8),
                              win_t[:, DIN + CONVD:DIP].rearrange("(k p) c -> p k c", p=128), "gld")
                wtok("tensor", v)
                if last_wave:
                    wtok("tensor", vdt)
                for j in range(nchk + (1 if last_wave else 0)):
                    bwait("tensor", bank0 + j)
                wait("tensor", "dv")
                for k in range(8):
                    for j in range(nchk):
                        mm(pb[bank0 + j][:, 0:width],
                           buf[:, k * ncols + j * 128:k * ncols + (j + 1) * 128],
                           a_bn1fm[:, k * slab:k * slab + width],
                           start=(k == 0), stop=(k == 7))
                    if last_wave:
                        mm(pb[bank0 + nchk][0:NH, 0:width],
                           a_wdt[:, k * NH:(k + 1) * NH],
                           a_bn1fm[:, k * slab:k * slab + width],
                           start=(k == 0), stop=(k == 7))
                mv = cnt["mm"]
                tk[("wv", par)] = ("mm", mv)
                wait("scalar", "mm", mv)
                if wv_i == 0:
                    wait("scalar", "dv")
                    wait("scalar", "mm")
                for j in range(nchk):
                    c = w0 + j
                    if kind == "z":
                        act(a_zsf[:, c * slab:c * slab + width], pb[bank0 + j][:, 0:width], AF.Silu)
                    else:
                        act(a_xbc[:, c * (slab + 3) + 3:c * (slab + 3) + 3 + width],
                            pb[bank0 + j][:, 0:width], AF.Copy)
                    bset(bank0 + j, "ac")
                if last_wave:
                    act(a_dt[0:NH, 0:width], pb[bank0 + nchk][0:NH, 0:width], AF.Exp, bias=c_dtb[:, :])
                    bset(bank0 + nchk, "ac")
                    act(a_dt[0:NH, 0:width], a_dt[0:NH, 0:width], AF.Ln, bias=1.0)

        def conv_slab(width):
            wait("scalar", "mm")
            wait("vector", "ac")
            for c in range(NCC):
                base = c * (slab + 3)
                ob = c * slab
                w = c_convw
                cu = a_cu[:, (c % 2) * 512:(c % 2) * 512 + width]
                if c >= 2:
                    wait("vector", "ac")
                ts(cu, a_xbc[:, base + 0:base + width], w[:, c * 4 + 0:c * 4 + 1])
                stt(cu, a_xbc[:, base + 1:base + 1 + width],
                    w[:, c * 4 + 1:c * 4 + 2], cu, OP.mult, OP.add)
                stt(cu, a_xbc[:, base + 2:base + 2 + width],
                    w[:, c * 4 + 2:c * 4 + 3], cu, OP.mult, OP.add)
                stt(cu, a_xbc[:, base + 3:base + 3 + width],
                    w[:, c * 4 + 3:c * 4 + 4], cu, OP.mult, OP.add)
                dv = cnt["dv"]
                wait("scalar", "dv", dv)
                act(a_xbco[:, ob:ob + width], cu, AF.Silu, bias=c_convb[:, c:c + 1])
            tk["xbco"] = ("ac", cnt["ac"])
            wait("vector", "ac")
            op("vector", lambda E: E.tensor_copy(
                r3(a_xbc, NCC, slab + 3)[:, :, 0:3],
                r3(a_xbc, NCC, slab + 3)[:, :, width:width + 3]), "dv")

        def scan_chunk(s, c):
            g = s * ncs + c
            t0 = c * Q
            lsl = slice(c * NH, (c + 1) * NH)
            par = g % 2
            xh_ = a_xhtm[:, par * DIN:(par + 1) * DIN]
            xw_ = a_xw[:, par * DIN:(par + 1) * DIN]
            xw2_ = a_xw2[:, par * DIN:(par + 1) * DIN]
            dxh_ = xw2_
            y1_ = xw_
            btm_ = a_btm[:, par * 128:(par + 1) * 128]
            gc_ = a_gc[:, par * 128:(par + 1) * 128]
            np_ = a_np[:, par * NH * 128:(par + 1) * NH * 128]
            srcs = [(a_xbco[:, cc * slab + t0:cc * slab + t0 + Q], xh_[:, cc * 128:(cc + 1) * 128])
                    for cc in range(16)]
            srcs.append((a_xbco[:, 16 * slab + t0:16 * slab + t0 + Q], btm_))
            for w0 in range(0, len(srcs), 4):
                grp = srcs[w0:w0 + 4]
                for j in range(len(grp)):
                    bwait("tensor", 4 + j)
                if w0 == 0:
                    wtok("tensor", tk.get(("scan_sb", par)))
                    wtok("tensor", tk.get("xbco"))
                for j, (srcap, dstap) in enumerate(grp):
                    tr(pbb[4 + j][:, 0:128], srcap, c_eyeb[:, :])
                mv = cnt["mm"]
                wait("vector", "mm", mv)
                for j, (srcap, dstap) in enumerate(grp):
                    cp(dstap, pbb[4 + j][:, 0:128])
                    bset(4 + j, "dv")
            cp(p_Csm[:, g * 128:(g + 1) * 128], a_xbco[:, 17 * slab + t0:17 * slab + t0 + Q])
            wait("vector", "ac")
            tt(xw_, xh_, bco(a_dtmb[:, lsl], HD))
            tt(xw2_, xw_, bco(a_wcol[:, lsl], HD))
            bwait("tensor", 6)
            mm(pb[6][:, 0:128], a_xbco[:, 16 * slab + t0:16 * slab + t0 + Q],
               a_xbco[:, 17 * slab + t0:17 * slab + t0 + Q], True, True)
            mv = cnt["mm"]
            wait("vector", "mm", mv)
            tt(gc_, pb[6][:, 0:128], c_triub[:, :])
            bset(6, "dv")
            for j in range(4):
                bwait("tensor", j)
            dv = cnt["dv"]
            wait("tensor", "dv", dv)
            for j in range(4):
                mm(pb[j][:, 0:512], btm_, xw2_[:, j * 512:(j + 1) * 512], True, True)
            mvS = cnt["mm"]
            wtok("vector", tk.get("srb"))
            cp(a_srb[:, :], p_Srun[:, :])
            tt(r3(p_Srun, NH, HD), r3(p_Srun, NH, HD), bco(p_dab[:, g * NH:(g + 1) * NH], HD))
            wait("vector", "mm", mvS)
            for j in range(4):
                tt(p_Srun[:, j * 512:(j + 1) * 512], p_Srun[:, j * 512:(j + 1) * 512],
                   pb[j][:, 0:512], OP.add)
                bset(j, "dv")
            tt(dxh_, xh_, c_dtile[:, :])
            wtok("vector", tk.get("zs"))
            for half in range(2):
                wtok("vector", tk.get("d2"))
                tt(r3(d2_t, NH // 2, 128),
                   bco(a_deltm[:, c * NH + half * 16:c * NH + (half + 1) * 16], 128),
                   bci(c_triu[:, :], NH // 2))
                for j in range(4):
                    bwait("tensor", 4 + j)
                wait("tensor", "dv")
                for j in range(4):
                    mm(pb[4 + j][:, 0:512], c_ptl[:, :],
                       d2_t[:, j * 512:(j + 1) * 512], True, True)
                mv = cnt["mm"]
                tk["d2"] = ("mm", mv)
                wtok("scalar", tk.get("wp"))
                wait("scalar", "mm", mv)
                for j in range(4):
                    act(a_wp[:, j * 512:(j + 1) * 512], pb[4 + j][:, 0:512], AF.Exp)
                    bset(4 + j, "ac")
                acv = cnt["ac"]
                wait("vector", "ac", acv)
                tt(r3(np_[:, half * 2048:(half + 1) * 2048], NH // 2, 128),
                   r3(a_wp[:, 0:2048], NH // 2, 128), bci(gc_, NH // 2))
                tk["wp"] = ("dv", cnt["dv"])
            dv = cnt["dv"]
            for j in range(4):
                bwait("tensor", j)
            wait("tensor", "dv", dv)
            for j in range(4):
                mm(pb[j][:, 0:512], c_eyeb[:, :], dxh_[:, j * 512:(j + 1) * 512], True, False)
            for h in range(NH):
                j = h // 8
                mm(pb[j][:, (h % 8) * 64:(h % 8) * 64 + 64],
                   np_[:, h * 128:(h + 1) * 128], xw_[:, h * 64:(h + 1) * 64],
                   False, h % 8 == 7)
            for j in range(4):
                bwait("tensor", 4 + j)
            for j in range(4):
                mm(pb[4 + j][:, 0:512], p_Csm[:, g * 128:(g + 1) * 128],
                   a_srb[:, j * 512:(j + 1) * 512], True, True)
            mv = cnt["mm"]
            tk["srb"] = ("mm", mv)
            wait("vector", "mm", mv)
            for j in range(4):
                tt(y1_[:, j * 512:(j + 1) * 512].rearrange("p (h e) -> p h e", h=8),
                   r3(pb[4 + j][:, 0:512], 8, 64),
                   bco(p_expl[:, g * NH + j * 8:g * NH + (j + 1) * 8], HD))
                bset(4 + j, "dv")
            wtok("vector", tk.get("ypre"))
            for j in range(4):
                tt(a_ypre[:, j * 512:(j + 1) * 512], y1_[:, j * 512:(j + 1) * 512],
                   pb[j][:, 0:512], OP.add)
                bset(j, "dv")
            dv = cnt["dv"]
            wait("gpsimd", "dv", dv)
            tk["ypre"] = dma("gpsimd", st_ypre[g, :, :], a_ypre[:, :], "gld")
            tk[("scan_sb", par)] = ("dv", dv)

        def subtiles(s):
            nsub = slab // 128
            toks = []
            for i in range(nsub):
                wtok("sync", tk.get(("xc", i)))
                toks.append(dma("sync", a_x[:, i * D:(i + 1) * D],
                                xin[128 + s * slab + i * 128:128 + s * slab + (i + 1) * 128, :]))
            for i in range(nsub):
                xap = a_x[:, i * D:(i + 1) * D]
                wtok("vector", toks[i])
                stt(a_xn[:, :], xap, 1.0, xap, OP.mult, OP.bypass, accum=p_stats[:, i:i + 1])
                stt(a_xn[:, :], xap, 1.0, xap, OP.mult, OP.mult, accum=p_stats[:, 4 + i:5 + i])
            m4 = p_rr[:, 0:nsub]
            v4 = p_rr[:, 4:4 + nsub]
            rstd = p_rr[:, 8:8 + nsub]
            r2 = p_rr[:, 12:12 + nsub]
            sb_ = p_rr[:, 16:16 + nsub]
            sc2 = p_rr[:, 20:20 + nsub]
            sb2 = p_rr[:, 24:24 + nsub]
            s0 = p_rr[:, 28:28 + nsub]
            s1 = p_rr[:, 32:32 + nsub]
            ts(m4, p_stats[:, 0:nsub], 1.0 / D)
            ts(v4, p_stats[:, 4:4 + nsub], 1.0 / D)
            tt(s0, m4, m4)
            tt(v4, v4, s0, OP.subtract)
            ts(s0, v4, 1e-5, o1=OP.add)
            rsqrt_newton(rstd, s0, s1, p_rr[:, 36:36 + nsub])
            tt(r2, rstd, rstd)
            tt(r2, r2, v4)
            ts(r2, r2, 1024.0, 1e-6, OP.mult, OP.add)
            rsqrt_newton(r2, r2, s1, p_rr[:, 36:36 + nsub])
            tt(sb_, m4, rstd)
            ts(sb_, sb_, -1.0)
            tt(sc2, rstd, r2)
            tt(sb2, sb_, r2)
            for i in range(nsub):
                subtile(128 + s * slab + i * 128, i, i, pre=False, tok=toks[i],
                        st=(rstd[:, i:i + 1], sb_[:, i:i + 1], sc2[:, i:i + 1], sb2[:, i:i + 1]))

        def delta_l(s):
            wait("vector", "ac")
            wait("vector", "mm")
            ts(a_dl[0:NH, 0:slab], a_dt[0:NH, 0:slab], c_negA[:, :])
            for c in range(ncs):
                op("vector", lambda E, c=c: E.tensor_tensor_scan(
                    out=a_l[0:NH, c * Q:(c + 1) * Q], data0=a_dl[0:NH, c * Q:(c + 1) * Q],
                    data1=a_dl[0:NH, c * Q:(c + 1) * Q], initial=0.0,
                    op0=OP.add, op1=OP.bypass), "dv")
            for c in range(ncs):
                bwait("tensor", c)
                bwait("tensor", 4 + c)
            wait("tensor", "dv")
            for c in range(ncs):
                tr(pb[c][0:128, 0:NH], a_l[0:NH, c * Q:(c + 1) * Q], c_eye[0:NH, 0:NH])
                tr(pb[4 + c][0:128, 0:NH], a_dt[0:NH, c * Q:(c + 1) * Q], c_eye[0:NH, 0:NH])
            mv = cnt["mm"]
            wait("vector", "mm", mv)
            for c in range(ncs):
                cp(a_ltm[:, c * NH:(c + 1) * NH], pb[c][:, 0:NH])
                cp(a_dtmb[:, c * NH:(c + 1) * NH], pb[4 + c][:, 0:NH])
                bset(c, "dv")
                tt(a_deltm[:, c * NH:(c + 1) * NH], pb[4 + c][:, 0:NH], c_negAt[:, :])
                bset(4 + c, "dv")
            dv = cnt["dv"]
            wait("scalar", "dv", dv)
            act(p_expl[:, s * ncs * NH:(s + 1) * ncs * NH], a_ltm[:, 0:ncs * NH], AF.Exp)
            wait("sync", "dv", dv)
            vr = dma("sync", a_lrow[0:1, 0:ncs * NH], a_ltm[127:128, 0:ncs * NH])
            wtok("vector", vr)
            for c in range(ncs):
                tt(p_sumL[0:1, :], p_sumL[0:1, :], a_lrow[0:1, c * NH:(c + 1) * NH], OP.add)
            wtok("tensor", vr)
            bwait("tensor", 6)
            mm(pb[6][:, 0:ncs * NH], c_onesr[0:1, :], a_lrow[0:1, 0:ncs * NH], True, True)
            mv = cnt["mm"]
            wait("vector", "mm", mv)
            cp(a_lendb[:, 0:ncs * NH], pb[6][:, 0:ncs * NH])
            wait("scalar", "mm", mv)
            act(p_dab[:, s * ncs * NH:(s + 1) * ncs * NH], pb[6][:, 0:ncs * NH], AF.Exp)
            bset(6, "ac")
            tt(a_lendb[:, 0:ncs * NH], a_lendb[:, 0:ncs * NH], a_ltm[:, 0:ncs * NH], OP.subtract)
            dv = cnt["dv"]
            wait("scalar", "dv", dv)
            act(a_wcol[:, 0:ncs * NH], a_lendb[:, 0:ncs * NH], AF.Exp)

        def zs_stash(s):
            wait("gpsimd", "ac")
            tk["zs"] = dma("gpsimd", st_zs[s, :, :], a_zsf[:, :], "gld")

        subtile(0, 0, 0, pre=True)
        inproj_waves(128, pre=True)
        wait("vector", "ac")
        op("vector", lambda E: E.tensor_copy(
            a_xbc.rearrange("p (c w) -> p c w", c=NCC)[:, :, 0:3],
            a_xbc.rearrange("p (c w) -> p c w", c=NCC)[:, :, 128:131]), "dv")
        subtiles(0)
        inproj_waves(slab, pre=False)
        inproj_waves(slab, pre="z")
        tk["bn1fm"] = ("mm", cnt["mm"])
        delta_l(0)
        conv_slab(slab)
        zs_stash(0)
        for s in range(nslab):
            nxt = s + 1 < nslab
            scan_chunk(s, 0)
            if ncs > 1:
                scan_chunk(s, 1)
            if nxt:
                subtiles(s + 1)
            for c in range(2, ncs):
                scan_chunk(s, c)
            if nxt:
                inproj_waves(slab, pre=False)
                inproj_waves(slab, pre="z")
                tk["bn1fm"] = ("mm", cnt["mm"])
                delta_l(s + 1)
                conv_slab(slab)
                zs_stash(s + 1)

        phase_barrier()
        wait("tensor", "dv")
        bwait("tensor", 0)
        tr(pb[0][0:1, 0:128], p_maxxn[:, :], c_eye[:, :])
        mv = cnt["mm"]
        wait("vector", "mm", mv)
        red(p_sc[0:1, 13:14], pb[0][0:1, 0:128], OP.max, absv=False)
        bset(0, "dv")
        dv = cnt["dv"]
        wait("gpsimd", "dv", dv)
        zw = CC1W - DIN
        zbuf = ar_f[:, 0:zw]
        op("vector", lambda E: E.memset(zbuf, 0.0), "dv")
        dv = cnt["dv"]
        wait("gpsimd", "dv", dv)
        zt = dma("gpsimd", cc1_in[:, DIN:CC1W], zbuf, "gld")
        dma("gpsimd", cc1_in[:, 0:DIN], p_Srun[:, :], "gld")
        wtok("gpsimd", zt)
        dma("gpsimd", cc1_in[0:1, DIN:DIN + NH], p_sumL[0:1, :], "gld")
        dma("gpsimd", cc1_in[0:1, DIN + NH:DIN + NH + 1], p_sc[0:1, 13:14], "gld")
        wq("gpsimd", "gld")
        op("gpsimd", lambda E: E.collective_compute(
            "AllGather", OP.bypass, replica_groups=[list(range(8))],
            ins=[cc1_in.opt()], outs=[cc1_out.opt()]), "cc", 1)
        wait("gpsimd", "cc", cnt["cc"])
        t1_ = dma("gpsimd", p_w8[0:8, 0:NH], cc1_out[:, 0, DIN:DIN + NH], "gld")
        t2_ = dma("gpsimd", p_w8r[0:1, 0:8], cc1_out[:, 0, DIN + NH:DIN + NH + 1], "gld", slow=True)
        wtok("vector", t1_)
        wtok("vector", t2_)
        red(p_sc[0:1, 4:5], p_w8r[0:1, 0:8], OP.max, absv=False)
        ts(p_sc[0:1, 5:6], p_sc[0:1, 4:5], 1e-6, 1.0 / 127.0, OP.max, OP.mult)
        op("vector", lambda E: E.reciprocal(p_sc[0:1, 6:7], p_sc[0:1, 5:6]), "dv")
        tt(p_sc[0:1, 10:11], p_sc[0:1, 0:1], p_sc[0:1, 5:6])
        bcast_col(p_rq1[:, 0:1], p_sc[0:1, 6:7])
        bcast_col(p_fs1[:, 0:1], p_sc[0:1, 10:11])
        ts(p_b1s[:, :], c_fb1[:, :], p_fs1[:, 0:1])
        wtok("tensor", t1_)
        bwait("tensor", 1)
        mm(pb[1][0:8, 0:NH], c_umat[0:8, 0:8], p_w8[0:8, 0:NH], True, True)
        mv = cnt["mm"]
        wait("scalar", "mm", mv)
        act(p_w8[0:8, 0:NH], pb[1][0:8, 0:NH], AF.Exp)
        bset(1, "ac")
        wait("vector", "ac")
        ts(p_w8[0:8, 0:NH], p_w8[0:8, 0:NH], c_gmask[0:8, 0:1])
        dv = cnt["dv"]
        wait("sync", "dv", dv)
        v = dma("sync", p_w8r[0:1, 0:8 * NH], p_w8[0:8, 0:NH])
        CM = Alloc()
        p_stmp = CM.af(DIN)
        p_Sin = CM.af(DIN)
        op("vector", lambda E: E.memset(p_Sin[:, :], 0.0), "dv")
        wq("sync", "gld")
        wtok("tensor", v)
        for j in range(8):
            bwait("tensor", 2)
            mm(pb[2][:, 0:NH], c_onesr[0:1, :], p_w8r[0:1, j * NH:(j + 1) * NH], True, True)
            mv = cnt["mm"]
            v = dma("sync", p_stmp[:, 0:DIN], cc1_out[j, :, 0:DIN])
            wait("vector", "mm", mv)
            cp(p_wbj[:, :], pb[2][:, 0:NH])
            bset(2, "dv")
            wtok("vector", v)
            tt(r3(p_stmp, NH, HD), r3(p_stmp, NH, HD), bco(p_wbj[:, :], HD))
            tt(p_Sin[:, :], p_Sin[:, :], p_stmp[:, :], OP.add)
            wait("sync", "dv")
            wait("tensor", "dv")
        cp(p_Sprop[:, :], p_Sin[:, :])

        phase_barrier()
        B1 = Alloc()
        b_wout = B1.abf(16 * D)
        b_ypre4 = B1.abf(ncs * DIN)
        b_zsf = B1.abf(16 * slab)
        b_zs = B1.abf(DIN)
        b_y1 = B1.abf(DIN)
        b_yg = B1.abf(DIN)
        b_yn = B1.abf(DIN)
        b_sq = b_yn
        b_yfm = B1.abf(16 * slab)
        b_xattnb = B1.abf(8 * slab)
        b_xsq = b_yfm[:, 0:8 * slab]
        b_hin = B1.abf(8 * slab)
        b_wstage = B1.af(D)
        b_xnfm = B1.af(8 * slab)
        b_xattn = B1.af(8 * slab)
        b_ssq = B1.af(4)
        p_row = B1.af(4 * slab)[0:1, :]
        assert B1.bf <= AB_N and B1.f <= AF_N, (B1.bf, B1.f)

        for c in range(16):
            wait("sync", "dv")
            v = dma("sync", b_wstage[:, 0:D], wout_t[c * 128:(c + 1) * 128, :])
            wtok("vector", v)
            ts(b_wout[:, c * D:(c + 1) * D], b_wstage[:, 0:D], c_mnw[:, c:c + 1])

        b1_prev = None
        for s in range(nslab):
            if b1_prev is None:
                wait("sync", "dv")
            else:
                zsf_mm_p, ypre_dv_p, xnfm_dv_p = b1_prev
                wait("sync", "mm", zsf_mm_p)
                wait("sync", "dv", ypre_dv_p)
            vzf = dma("sync", b_zsf[:, :], st_zs[s, :, :])
            vy = dma("sync", b_ypre4[:, :].rearrange("p (c t) -> p c t", c=ncs),
                     st_ypre[s * ncs:(s + 1) * ncs].rearrange("c p t -> p c t"))
            if b1_prev is not None:
                wait("sync", "dv", xnfm_dv_p)
            v = dma("sync", b_xnfm[:, :], st_xnfm[s, :, :])
            for c in range(ncs):
                g = s * ncs + c
                b_ypre = b_ypre4[:, c * DIN:(c + 1) * DIN]
                wtok("tensor", vzf)
                for w0 in range(0, 16, 4):
                    for j in range(4):
                        bwait("tensor", 4 + j)
                    wait("tensor", "dv")
                    for j in range(4):
                        tr(pbb[4 + j][:, 0:128],
                           b_zsf[:, (w0 + j) * slab + c * 128:(w0 + j) * slab + (c + 1) * 128],
                           c_eyeb[:, :])
                    mv = cnt["mm"]
                    wait("vector", "mm", mv)
                    for j in range(4):
                        cp(b_zs[:, (w0 + j) * 128:(w0 + j + 1) * 128], pbb[4 + j][:, 0:128])
                        bset(4 + j, "dv")
                wait("tensor", "dv")
                for j in range(4):
                    bwait("tensor", j)
                for j in range(4):
                    mm(pb[j][:, 0:512], p_Csm[:, g * 128:(g + 1) * 128],
                       p_Sprop[:, j * 512:(j + 1) * 512], True, True)
                mv = cnt["mm"]
                wait("vector", "mm", mv)
                wtok("vector", vy)
                for j in range(4):
                    tt(b_y1[:, j * 512:(j + 1) * 512].rearrange("p (h e) -> p h e", h=8),
                       r3(pb[j][:, 0:512], 8, 64),
                       bco(p_expl[:, g * NH + j * 8:g * NH + (j + 1) * 8], HD))
                    bset(j, "dv")
                tt(b_y1[:, :], b_y1[:, :], b_ypre[:, :], OP.add)
                tt(b_yg[:, :], b_y1[:, :], b_zs[:, :])
                stt(b_sq[:, :], b_yg[:, :], 1.0, b_yg[:, :], OP.mult, OP.mult,
                    accum=b_ssq[:, 0:1])
                tt(r3(p_Sprop, NH, HD), r3(p_Sprop, NH, HD),
                   bco(p_dab[:, g * NH:(g + 1) * NH], HD))
                ts(b_ssq[:, 1:2], b_ssq[:, 0:1], 1.0 / DIN, 1e-5, OP.mult, OP.add)
                rsqrt_newton(b_ssq[:, 2:3], b_ssq[:, 1:2], p_rr[:, 9:10], p_rr[:, 10:11])
                ts(b_yn[:, :], b_yg[:, :], b_ssq[:, 2:3])
                for w0 in range(0, 16, 4):
                    for j in range(4):
                        bwait("tensor", 4 + j)
                    wait("tensor", "dv")
                    for j in range(4):
                        tr(pbb[4 + j][:, 0:128], b_yn[:, (w0 + j) * 128:(w0 + j + 1) * 128],
                           c_eyeb[:, :])
                    mv = cnt["mm"]
                    wait("vector", "mm", mv)
                    for j in range(4):
                        cp(b_yfm[:, (w0 + j) * slab + c * 128:(w0 + j) * slab + (c + 1) * 128],
                           pbb[4 + j][:, 0:128])
                        bset(4 + j, "dv")
            zsf_mm_s = cnt["mm"]
            ypre_dv_s = cnt["dv"]
            wait("tensor", "dv")
            for j in range(8):
                bwait("tensor", j)
            for k in range(16):
                for dc in range(8):
                    mm(pb[dc][:, 0:slab],
                       b_wout[:, k * D + dc * 128:k * D + (dc + 1) * 128],
                       b_yfm[:, k * slab:(k + 1) * slab],
                       start=(k == 0), stop=(k == 15))
            mv = cnt["mm"]
            wait("vector", "mm", mv)
            wtok("vector", v)
            wq("vector", "gld")
            for dc in range(8):
                tt(b_xattn[:, dc * slab:(dc + 1) * slab], pb[dc][:, 0:slab],
                   b_xnfm[:, dc * slab:(dc + 1) * slab], OP.add)
                bset(dc, "dv")
            wtok("vector", tk.get("xattn"))
            cp(b_xattnb[:, :], b_xattn[:, :])
            dv = cnt["dv"]
            b1_prev = (zsf_mm_s, ypre_dv_s, dv)
            wait("gpsimd", "dv", dv)
            tk["xattn"] = dma("gpsimd", st_xattn[s, :, :], b_xattnb[:, :], "gld")
            wait("scalar", "dv", dv)
            for dc in range(8):
                act(b_xsq[:, dc * slab:(dc + 1) * slab], b_xattn[:, dc * slab:(dc + 1) * slab],
                    AF.Square)
            acv = cnt["ac"]
            wait("tensor", "ac", acv)
            bwait("tensor", 0)
            mm(pb[0][0:1, 0:slab], c_onescb[:, 0:1], b_xsq[:, 0:slab], True, False)
            for dc in range(1, 8):
                mm(pb[0][0:1, 0:slab], c_onescb[:, 0:1], b_xsq[:, dc * slab:(dc + 1) * slab],
                   False, dc == 7)
            mv = cnt["mm"]
            wait("vector", "mm", mv)
            ts(p_row[0:1, 0:slab], pb[0][0:1, 0:slab], 1.0, 1e-6, OP.mult, OP.add)
            bset(0, "dv")
            rsqrt_newton(p_row[0:1, slab:2 * slab], p_row[0:1, 0:slab],
                         p_row[0:1, 2 * slab:3 * slab], p_row[0:1, 3 * slab:4 * slab])
            dv = cnt["dv"]
            wait("tensor", "dv", dv)
            bwait("tensor", 1)
            mm(pb[1][:, 0:slab], c_onesr[0:1, :], p_row[0:1, slab:2 * slab], True, True)
            mv = cnt["mm"]
            wait("vector", "mm", mv)
            wtok("vector", tk.get("hin"))
            for dc in range(8):
                tt(b_hin[:, dc * slab:(dc + 1) * slab], b_xattn[:, dc * slab:(dc + 1) * slab],
                   pb[1][:, 0:slab])
                bset(1, "dv") if dc == 7 else None
            dv = cnt["dv"]
            wait("gpsimd", "dv", dv)
            tk["hin"] = dma("gpsimd", st_hin[s, :, :], b_hin[:, :], "gld")

        phase_barrier()
        B2 = Alloc()
        b2_w1h = B2.abf(8 * 2048)
        b2_hin = B2.abf(8 * slab)
        b2_gel = B2.abf(32 * slab)
        b2_w2 = B2.abf(2 * 32 * 128)
        b2_xat = B2.abf(8 * slab)
        b2_att = B2.abf(8 * slab)
        assert B2.bf <= AB_N and B2.f <= AF_N, (B2.bf, B2.f)

        b2_prev = None
        for s in range(nslab):
            if b2_prev is None:
                wait("sync", "dv")
                wait("sync", "mm")
            else:
                hin_mm_p, xat_dv_p = b2_prev
                wait("sync", "mm", hin_mm_p)
            vh = dma("sync", b2_hin[:, :], st_hin[s, :, :])
            if b2_prev is not None:
                wait("sync", "dv", xat_dv_p)
            vx = dma("sync", b2_xat[:, :], st_xattn[s, :, :])
            for half in range(2):
                wait("sync", "mm")
                vw = dma("sync", b2_w1h[:, :].rearrange("p (k c) -> p k c", k=8),
                         sc_w1f.rearrange("p (k c) -> p k c", k=8)[:, :, half * 2048:(half + 1) * 2048])
                wtok("tensor", vh)
                wtok("tensor", vw)
                wait("tensor", "dv")
                for mi in range(16):
                    m = half * 16 + mi
                    bank = mi % 8
                    bwait("tensor", bank)
                    for k in range(8):
                        mm(pb[bank][:, 0:slab],
                           b2_w1h[:, k * 2048 + mi * 128:k * 2048 + (mi + 1) * 128],
                           b2_hin[:, k * slab:(k + 1) * slab],
                           start=(k == 0), stop=(k == 7))
                    mv = cnt["mm"]
                    wait("scalar", "mm", mv)
                    act(b2_gel[:, m * slab:(m + 1) * slab], pb[bank][:, 0:slab], AF.Gelu,
                        bias=c_b1[:, m:m + 1])
                    bset(bank, "ac")
            hin_mm_s = cnt["mm"]
            wait("tensor", "ac")
            wtok("scalar", tk.get("attno"))
            w2tok = {}

            def w2load(dc):
                buf_ = b2_w2[:, (dc % 2) * 32 * 128:(dc % 2) * 32 * 128 + 32 * 128]
                wtok("gpsimd", tk.get(("w2b", dc % 2)))
                w2tok[dc] = dma("gpsimd", buf_, sc_w2f[dc], "gld")

            w2load(0)
            for dc in range(8):
                if dc + 1 < 8:
                    w2load(dc + 1)
                buf = b2_w2[:, (dc % 2) * 32 * 128:(dc % 2) * 32 * 128 + 32 * 128]
                wtok("tensor", w2tok[dc])
                bwait("tensor", dc % 4)
                for k in range(32):
                    mm(pb[dc % 4][:, 0:slab], buf[:, k * 128:(k + 1) * 128],
                       b2_gel[:, k * slab:(k + 1) * slab], start=(k == 0), stop=(k == 31))
                mv = cnt["mm"]
                tk[("w2b", dc % 2)] = ("mm", mv)
                wait("scalar", "mm", mv)
                act(b2_att[:, dc * slab:(dc + 1) * slab], pb[dc % 4][:, 0:slab], AF.Identity,
                    bias=c_b2[:, dc:dc + 1])
                bset(dc % 4, "ac")
            acv = cnt["ac"]
            wait("vector", "ac", acv)
            wtok("vector", vx)
            wtok("scalar", tk.get("attno"))
            wait("vector", "ac")
            tt(b2_att[:, :], b2_att[:, :], b2_xat[:, :], OP.add)
            dv = cnt["dv"]
            b2_prev = (hin_mm_s, dv)
            wait("gpsimd", "dv", dv)
            tk["attno"] = dma("gpsimd", st_attno[s, :, :], b2_att[:, :], "gld")

        phase_barrier()
        B3 = Alloc()
        b3_w1q = B3.abf(8 * HMLP)
        b3_xq = B3.abf(8 * slab)
        b3_h = B3.abf(32 * slab)
        b3_xn = B3.af(8 * slab)
        b3_t1 = B3.af(2048)
        b3_ge = B3.af(2048)
        b3_int = B3.af(2048).bitcast(I32)
        assert B3.bf <= AB_N and B3.f <= AF_N, (B3.bf, B3.f)

        v = dma("sync", b3_w1q[:, :], sc_w1q[:, :])
        xn_dv_prev = None
        for s in range(nslab):
            if xn_dv_prev is None:
                wait("sync", "dv")
                wait("sync", "mm")
            else:
                wait("sync", "dv", xn_dv_prev)
            vx = dma("sync", b3_xn[:, :], st_xnfm[s, :, :])
            wtok("vector", vx)
            if s > 0:
                wait("vector", "mm")
            W8 = 8 * slab
            for h0 in range(0, W8, 2048):
                ts(b3_t1[:, 0:2048], b3_xn[:, h0:h0 + 2048], p_rq1[:, 0:1])
                quant_round(b3_xq[:, h0:h0 + 2048], b3_t1[:, 0:2048], 2048, b3_int, b3_ge)
            xn_dv_prev = cnt["dv"]
            dv = cnt["dv"]
            wait("tensor", "dv", dv)
            wtok("tensor", v)
            wtok("scalar", tk.get("h"))
            for m in range(32):
                bank = m % 8
                bwait("tensor", bank)
                for k in range(8):
                    mm(pb[bank][:, 0:slab],
                       b3_w1q[:, k * HMLP + m * 128:k * HMLP + (m + 1) * 128],
                       b3_xq[:, k * slab:(k + 1) * slab], start=(k == 0), stop=(k == 7))
                mv = cnt["mm"]
                wait("scalar", "mm", mv)
                act(b3_h[:, m * slab:(m + 1) * slab], pb[bank][:, 0:slab], AF.Gelu,
                    scale=p_fs1[:, 0:1], bias=p_b1s[:, m:m + 1])
                bset(bank, "ac")
            acv = cnt["ac"]
            wait("vector", "ac", acv)
            red(p_rr[:, 11:12], b3_h[:, 0:32 * slab])
            tt(p_maxh[:, :], p_maxh[:, :], p_rr[:, 11:12], OP.max)
            wait("gpsimd", "ac", acv)
            tk["h"] = dma("gpsimd", st_h[s, :, :], b3_h[:, :], "gld")

        phase_barrier()
        wait("tensor", "dv")
        bwait("tensor", 0)
        tr(pb[0][0:1, 0:128], p_maxh[:, :], c_eye[:, :])
        mv = cnt["mm"]
        wait("vector", "mm", mv)
        red(p_sc[0:1, 14:15], pb[0][0:1, 0:128], OP.max, absv=False)
        bset(0, "dv")
        dv = cnt["dv"]
        wait("gpsimd", "dv", dv)
        dma("gpsimd", cc2_in[0:1, 0:1], p_sc[0:1, 14:15], "gld")
        wq("gpsimd", "gld")
        op("gpsimd", lambda E: E.collective_compute(
            "AllGather", OP.bypass, replica_groups=[list(range(8))],
            ins=[cc2_in.opt()], outs=[cc2_out.opt()]), "cc", 1)
        wait("gpsimd", "cc", cnt["cc"])
        t3_ = dma("gpsimd", p_w8r[0:1, 0:8], cc2_out[:, 0:1], "gld")
        wtok("vector", t3_)
        red(p_sc[0:1, 7:8], p_w8r[0:1, 0:8], OP.max, absv=False)
        ts(p_sc[0:1, 8:9], p_sc[0:1, 7:8], 1e-6, 1.0 / 127.0, OP.max, OP.mult)
        op("vector", lambda E: E.reciprocal(p_sc[0:1, 9:10], p_sc[0:1, 8:9]), "dv")
        tt(p_sc[0:1, 11:12], p_sc[0:1, 2:3], p_sc[0:1, 8:9])
        bcast_col(p_rq2[:, 0:1], p_sc[0:1, 9:10])
        bcast_col(p_fs2[:, 0:1], p_sc[0:1, 11:12])
        ts(p_fb2s[:, :], c_fb2[:, :], p_fs2[:, 0:1])

        phase_barrier()
        C = Alloc()
        c_h = C.abf(32 * slab)
        c_hq = C.abf(32 * slab)
        c_w2q = C.abf(2 * 8 * 512)
        c_att = C.abf(8 * slab)
        c_tot = C.af(8 * slab)
        c_t1 = C.af(2048)
        c_ge2 = C.af(2048)
        c_int2 = c_tot.bitcast(I32)[:, 0:2048]
        c_xpx = c_t1[:, 0:D]
        c_out = C.af(D)
        assert C.bf <= AB_N and C.f <= AF_N, (C.bf, C.f)

        q_dv_prev = None
        att_dv_prev = None
        for s in range(nslab):
            if q_dv_prev is None:
                wait("sync", "dv")
                wait("sync", "mm")
            else:
                wait("sync", "dv", q_dv_prev)
            vh = dma("sync", c_h[:, :], st_h[s, :, :])
            if att_dv_prev is not None:
                wait("sync", "dv", att_dv_prev)
            va = dma("sync", c_att[:, :], st_attno[s, :, :])
            wtok("vector", vh)
            if s > 0:
                wait("vector", "mm")
            nstrip = (32 * slab) // 2048
            ksper = 32 // nstrip
            qdv = {}
            wtoks = {}

            def qstrip(st):
                sl = slice(st * 2048, (st + 1) * 2048)
                ts(c_t1[:, 0:2048], c_h[:, sl], p_rq2[:, 0:1])
                quant_round(c_hq[:, sl], c_t1[:, 0:2048], 2048, c_int2, c_ge2)
                qdv[st] = cnt["dv"]

            def wload(st):
                par = (st % 2) * 8 * 512
                wait("sync", "mm")
                wtoks[st] = dma("sync",
                    c_w2q[:, par:par + 8 * ksper * 128].rearrange("p (d c) -> p d c", d=8),
                    sc_w2q[:, :, st * ksper * 128:(st + 1) * ksper * 128]
                    .rearrange("d p c -> p d c"))

            wload(0)
            qstrip(0)
            for st in range(nstrip):
                if st + 1 < nstrip:
                    wload(st + 1)
                    qstrip(st + 1)
                par = (st % 2) * 8 * 512
                wtok("tensor", wtoks[st])
                wait("tensor", "dv", qdv[st])
                for dc in range(8):
                    if st == 0:
                        bwait("tensor", dc)
                    for kk in range(ksper):
                        k = st * ksper + kk
                        mm(pb[dc][:, 0:slab],
                           c_w2q[:, par + dc * ksper * 128 + kk * 128:par + dc * ksper * 128 + (kk + 1) * 128],
                           c_hq[:, k * slab:(k + 1) * slab],
                           start=(st == 0 and kk == 0), stop=(st == nstrip - 1 and kk == ksper - 1))
            mv = cnt["mm"]
            wait("scalar", "mm", mv)
            for dc in range(8):
                act(c_tot[:, dc * slab:(dc + 1) * slab], pb[dc][:, 0:slab], AF.Identity,
                    scale=p_fs2[:, 0:1], bias=p_fb2s[:, dc:dc + 1])
                bset(dc, "ac")
            acv = cnt["ac"]
            wait("vector", "ac", acv)
            wtok("vector", va)
            tt(c_tot[:, :], c_tot[:, :], c_att[:, :], OP.add)
            q_dv_prev = qdv[nstrip - 1]
            att_dv_prev = cnt["dv"]
            for i in range(slab // 128):
                wait("sync", "dv")
                vx = dma("sync", c_xpx[:, 0:D],
                         xin[128 + s * slab + i * 128:128 + s * slab + (i + 1) * 128, :])
                for w0 in range(0, 8, 4):
                    wait("tensor", "dv")
                    for j in range(4):
                        dck = w0 + j
                        tr(pb[4 + j][:, 0:128], c_tot[:, dck * slab + i * 128:dck * slab + (i + 1) * 128],
                           c_eye[:, :])
                    mv = cnt["mm"]
                    wait("vector", "mm", mv)
                    wtok("vector", vx)
                    if w0 == 0:
                        wtok("vector", tk.get("out"))
                    for j in range(4):
                        dck = w0 + j
                        tt(c_out[:, dck * 128:(dck + 1) * 128], pb[4 + j][:, 0:128],
                           c_xpx[:, dck * 128:(dck + 1) * 128], OP.add)
                dv = cnt["dv"]
                wait("gpsimd", "dv", dv)
                tk["out"] = dma("gpsimd", out[s * slab + i * 128:s * slab + (i + 1) * 128, :], c_out[:, 0:D], "ost")

        if self.debug_outs:
            stmap = {"ypre": st_ypre, "zs": st_zs, "xnfm": st_xnfm, "xattn": st_xattn,
                     "hin": st_hin, "attno": st_attno, "h": st_h, "cc1o": cc1_out,
                     "cc2o": cc2_out, "w1q": sc_w1q, "w2q": sc_w2q,
                     "psc": p_sc, "b1s": p_b1s, "fs1": p_fs1, "fs2": p_fs2,
                     "rq1": p_rq1, "rq2": p_rq2}
            wq("gpsimd", "sld")
            wq("gpsimd", "gld")
            wait("gpsimd", "dv")
            wait("gpsimd", "ac")
            wait("gpsimd", "mm")
            for name in self.debug_outs:
                src = stmap[name]
                d = nc.dram_tensor("dbg_" + name, list(src.shape), src.dtype,
                                   kind="ExternalOutput").ap()
                dma("gpsimd", d, src, "gld")
        wq("gpsimd", "ost")
        wq("gpsimd", "gld")

        with nc.Block() as block:
            @block.sync
            def _(E):
                for f in self.prog["sync"]:
                    f(E)

            @block.scalar
            def _(E):
                for f in self.prog["scalar"]:
                    f(E)

            @block.vector
            def _(E):
                for f in self.prog["vector"]:
                    f(E)

            @block.tensor
            def _(E):
                for f in self.prog["tensor"]:
                    f(E)

            @block.gpsimd
            def _(E):
                for f in self.prog["gpsimd"]:
                    f(E)
        return nc


def make_inputs(full, T, core):
    b, q = core // 4, core % 4
    L = full["L0_attn_x"].shape[1]
    Tq = L // 4
    assert Tq == T
    x = np.asarray(full["L0_attn_x"][b], np.float32)
    sh = np.zeros((128 + T, D), np.float32)
    t0 = q * Tq
    sh[128:, :] = x[t0:t0 + Tq]
    if q > 0:
        sh[0:128, :] = x[t0 - 128:t0]
    ones = np.ones((128, 128), np.float32)
    g = core // 4
    p = core % 4
    gm = np.zeros((8, 1), np.float32)
    um = np.zeros((8, 8), np.float32)
    for j in range(8):
        if j // 4 == g and j % 4 < p:
            gm[j, 0] = 1.0
            for r in range(8):
                if r // 4 == g and (j % 4) < (r % 4) < p:
                    um[j, r] = 1.0
    dt = lambda a: np.ascontiguousarray(np.asarray(a, np.float32))
    return {
        "xin": sh,
        "win_t": dt(full["L0_attn_in_proj_w"]).T.copy(),
        "wout_t": dt(full["L0_attn_out_proj_w"]).T.copy(),
        "w1_t": dt(full["L0_attn_mlp_w1"]).T.copy(),
        "w2_t": dt(full["L0_attn_mlp_w2"]).T.copy(),
        "fw1_t": dt(full["L0_ffn_w1"]).T.copy(),
        "fw2_t": dt(full["L0_ffn_w2"]).T.copy(),
        "convw": dt(full["L0_attn_conv_w"]).reshape(NCC, 128, 4).transpose(1, 0, 2).reshape(128, NCC * 4).copy(),
        "convb": dt(full["L0_attn_conv_b"]).reshape(NCC, 128).T.copy(),
        "norm1w": dt(full["L0_attn_norm1_w"]).reshape(8, 128).T.copy(),
        "norm2w": dt(full["L0_attn_norm2_w"]).reshape(8, 128).T.copy(),
        "mnormw": dt(full["L0_attn_mnorm_w"]).reshape(16, 128).T.copy(),
        "dtb": dt(full["L0_attn_dt_bias"]).reshape(-1, 1),
        "alog": dt(full["L0_attn_A_log"]).reshape(-1, 1),
        "alog_row": dt(full["L0_attn_A_log"]).reshape(1, -1),
        "dtile": np.tile(np.repeat(dt(full["L0_attn_D"]), HD)[None, :], (128, 1)).copy(),
        "b1": dt(full["L0_attn_mlp_b1"]).reshape(32, 128).T.copy(),
        "b2": dt(full["L0_attn_mlp_b2"]).reshape(8, 128).T.copy(),
        "fb1": dt(full["L0_ffn_b1"]).reshape(32, 128).T.copy(),
        "fb2": dt(full["L0_ffn_b2"]).reshape(8, 128).T.copy(),
        "eye_f": np.eye(128, dtype=np.float32),
        "triu_f": np.triu(ones).astype(np.float32),
        "ptl_f": np.tril(ones, -1).astype(np.float32),
        "ones_row": np.ones((1, 128), np.float32),
        "gmask": gm,
        "umat_t": um.T.copy(),
    }


def kernel(**inputs):
    T = inputs["L0_attn_x"].shape[1] // 4
    slab = 512 if T % 512 == 0 else T
    b = Builder(T, slab)
    nc = b.build()
    in_maps = [make_inputs(inputs, T, core) for core in range(8)]
    res = run_bass_kernel_spmd(nc, in_maps, core_ids=list(range(8)))
    B, L, _ = inputs["L0_attn_x"].shape
    outp = np.empty((B, L, D), np.float32)
    for core in range(8):
        bq, q = core // 4, core % 4
        outp[bq, q * T:(q + 1) * T, :] = res.results[core]["out"]
    return outp


# version 26
# speedup vs baseline: 1.1014x; 1.0042x over previous
import numpy as np
import concourse.bass as bass
import concourse.mybir as mybir
from concourse.bass_utils import run_bass_kernel_spmd

F32 = mybir.dt.float32
F32R = mybir.dt.float32r
BF16 = mybir.dt.bfloat16
I32 = mybir.dt.int32
AF = mybir.ActivationFunctionType
OP = mybir.AluOpType
AX = mybir.AxisListType.X

D = 1024
DIN = 2048
NH = 32
HD = 64
NSTATE = 128
CONVD = 2304
NCC = CONVD // 128
DIP = 4384
HMLP = 4096
Q = 128

ENGS = ("sync", "scalar", "vector", "tensor", "gpsimd")


class Builder:
    def __init__(self, T, slab, debug_outs=()):
        assert T % slab == 0 and slab % Q == 0
        self.T, self.slab = T, slab
        self.nslab = T // slab
        self.ncs = slab // Q
        self.nchunk = T // Q
        nc = self.nc = bass.Bass("TRN2", target_bir_lowering=False)
        self.prog = {e: [] for e in ENGS}
        self.sems = {}
        self.cnt = {}
        for s in ("mm", "dv", "ac", "cc"):
            self.sems[s] = nc.alloc_semaphore(s)
            self.cnt[s] = 0
        self._wmax = {}
        self.qpool = {}
        self.qidx = {}
        for q, npool in (("sld", 4), ("gld", 4), ("ost", 2)):
            names = []
            for i in range(npool):
                n = f"{q}{i}"
                self.sems[n] = nc.alloc_semaphore(n)
                self.cnt[n] = 0
                names.append(n)
            self.qpool[q] = names
            self.qidx[q] = 0
        self.debug_outs = list(debug_outs)

    def emit(self, eng, fn):
        self.prog[eng].append(fn)

    def wait(self, eng, s, v=None):
        v = self.cnt[s] if v is None else v
        if v > 0:
            self.emit(eng, lambda E, _s=self.sems[s], _v=v: E.wait_ge(_s, _v))

    _AUTOW = {"vector": ("dv",), "scalar": ("ac",)}

    def op(self, eng, fn, s=None, n=1):
        for sw in self._AUTOW.get(eng, ()):
            if self.cnt[sw] > 0:
                self.emit(eng, lambda E, _s=self.sems[sw], _v=self.cnt[sw]: E.wait_ge(_s, _v))
        if s is not None:
            self.cnt[s] += n
            self.emit(eng, lambda E, _f=fn, _s=self.sems[s], _n=n: _f(E).then_inc(_s, _n))
            return self.cnt[s]
        self.emit(eng, lambda E, _f=fn: _f(E))
        return None

    def dma(self, eng, outap, inap, s=None, slow=False):
        q = "gld" if (s is None and eng == "gpsimd") else ("sld" if s is None else s)
        pool = self.qpool[q]
        name = pool[self.qidx[q] % len(pool)]
        self.qidx[q] += 1
        self.wait(eng, name, self.cnt[name])
        if slow:
            v = self.op(eng, lambda E, _o=outap, _i=inap: E.dma_start(
                out=_o, in_=_i, allow_slow_non_contiguous=True), name, 16)
        else:
            v = self.op(eng, lambda E, _o=outap, _i=inap: E.dma_start(out=_o, in_=_i), name, 16)
        return (name, v)

    def wtok(self, eng, tok):
        if tok is None:
            return
        self.wait(eng, tok[0], tok[1])

    def wq(self, eng, q):
        for name in self.qpool[q]:
            self.wait(eng, name, self.cnt[name])

    def tt(self, out, a, b, o=OP.mult, s="dv", eng="vector"):
        return self.op(eng, lambda E: E.tensor_tensor(out=out, in0=a, in1=b, op=o), s)

    def ts(self, out, a, s1, s2=None, o1=OP.mult, o2=None, accum=None, eng="vector"):
        if o2 is None:
            return self.op(eng, lambda E: E.tensor_scalar(out=out, in0=a, scalar1=s1,
                                                          scalar2=None, op0=o1, accum_out=accum), "dv")
        return self.op(eng, lambda E: E.tensor_scalar(out=out, in0=a, scalar1=s1, scalar2=s2,
                                                      op0=o1, op1=o2, accum_out=accum), "dv")

    def stt(self, out, a, sc, b, o1, o2, accum=None):
        return self.op("vector", lambda E: E.scalar_tensor_tensor(
            out=out, in0=a, scalar=sc, in1=b, op0=o1, op1=o2, accum_out=accum), "dv")

    def cp(self, out, a, eng="vector"):
        return self.op(eng, lambda E: E.tensor_copy(out, a), "dv")

    def act(self, out, a, f, bias=None, scale=None):
        kw = {}
        if bias is not None:
            kw["bias"] = bias
        if scale is not None:
            kw["scale"] = scale
        return self.op("scalar", lambda E: E.activation(out=out, in_=a, func=f, **kw), "ac")

    def mm(self, psum, lhsT, rhs, start, stop):
        return self.op("tensor", lambda E: E.matmul(psum, lhsT, rhs, start=start, stop=stop), "mm")

    def tr(self, psum, data, ident):
        return self.op("tensor", lambda E: E.transpose(psum, data, ident), "mm")

    def red(self, out, a, o=OP.max, absv=True):
        return self.op("vector", lambda E: E.tensor_reduce(
            out=out, in_=a, axis=AX, op=o, apply_absolute_value=absv), "dv")

    def barrier(self, engs=("tensor", "vector", "scalar"), sems_=("mm", "dv", "ac")):
        for e in engs:
            for s in sems_:
                self.wait(e, s)

    @staticmethod
    def bco(ap2d, reps):
        P, K = ap2d.shape
        return ap2d.unsqueeze(2).broadcast_to((P, K, reps))

    @staticmethod
    def bci(ap2d, reps):
        P, K = ap2d.shape
        return ap2d.unsqueeze(1).broadcast_to((P, reps, K))

    @staticmethod
    def r3(ap2d, k, t):
        return ap2d.rearrange("p (k t) -> p k t", k=k)

    def build(self):
        nc, T, slab = self.nc, self.T, self.slab
        nslab, ncs, nchunk = self.nslab, self.ncs, self.nchunk

        def din(name, shape, dt=F32):
            return nc.dram_tensor(name, shape, dt, kind="ExternalInput").ap()

        xin = din("xin", [128 + T, D])
        win_t = din("win_t", [D, DIP])
        wout_t = din("wout_t", [DIN, D])
        w1_t = din("w1_t", [D, HMLP])
        w2_t = din("w2_t", [HMLP, D])
        fw1_t = din("fw1_t", [D, HMLP])
        fw2_t = din("fw2_t", [HMLP, D])
        convw = din("convw", [128, NCC * 4])
        convb = din("convb", [128, NCC])
        norm1w = din("norm1w", [128, 8])
        norm2w = din("norm2w", [128, 8])
        mnormw = din("mnormw", [128, 16])
        dtb = din("dtb", [NH, 1])
        alog = din("alog", [NH, 1])
        alog_row = din("alog_row", [1, NH])
        dtile = din("dtile", [128, DIN])
        b1 = din("b1", [128, 32])
        b2 = din("b2", [128, 8])
        fb1 = din("fb1", [128, 32])
        fb2 = din("fb2", [128, 8])
        eye_f = din("eye_f", [128, 128])
        triu_f = din("triu_f", [128, 128])
        ptl_f = din("ptl_f", [128, 128])
        ones_row = din("ones_row", [1, 128])
        gmask = din("gmask", [8, 1])
        umat_t = din("umat_t", [8, 8])

        out = nc.dram_tensor("out", [T, D], F32, kind="ExternalOutput").ap()

        def ddr(name, shape, dt):
            return nc.dram_tensor(name, shape, dt).ap()

        st_ypre = ddr("st_ypre", [nchunk, 128, DIN], BF16)
        st_zs = ddr("st_zs", [nslab, 128, 16 * slab], BF16)
        st_xnfm = ddr("st_xnfm", [nslab, 128, 8 * slab], F32)
        st_xpxn = ddr("st_xpxn", [T, D], BF16)
        st_xattn = ddr("st_xattn", [nslab, 128, 8 * slab], BF16)
        st_hin = ddr("st_hin", [nslab, 128, 8 * slab], BF16)
        st_attno = ddr("st_attno", [nslab, 128, 8 * slab], BF16)
        st_h = ddr("st_h", [nslab, 128, 32 * slab], BF16)
        sc_w1q = ddr("sc_w1q", [128, 8 * HMLP], BF16)
        sc_w2q = ddr("sc_w2q", [8, 128, 32 * 128], BF16)
        sc_w1f = ddr("sc_w1f", [128, 8 * HMLP], BF16)
        sc_w2f = ddr("sc_w2f", [8, 128, 32 * 128], BF16)
        CC1W = 2082
        cc1_in = ddr("cc1_in", [128, CC1W], F32)
        cc1_out = ddr("cc1_out", [8, 128, CC1W], F32)
        cc2_in = ddr("cc2_in", [1, 1], F32)
        cc2_out = ddr("cc2_out", [8, 1], F32)

        dbg_aps = {}
        for name in self.debug_outs:
            pass

        def sbuf(name, shape, dt):
            return nc.alloc_sbuf_tensor(name, shape, dt).ap()

        c_eye = sbuf("c_eye", [128, 128], F32)
        c_eyeb = sbuf("c_eyeb", [128, 128], BF16)
        c_triu = sbuf("c_triu", [128, 128], F32)
        c_triub = sbuf("c_triub", [128, 128], BF16)
        c_ptl = sbuf("c_ptl", [128, 128], F32R)
        c_onesr = sbuf("c_onesr", [1, 128], F32)
        c_onescb = sbuf("c_onescb", [128, 1], BF16)
        c_dtile = sbuf("c_dtile", [128, DIN], BF16)
        c_convw = sbuf("c_convw", [128, NCC * 4], F32)
        c_convb = sbuf("c_convb", [128, NCC], F32)
        c_n1w = sbuf("c_n1w", [128, 8], F32)
        c_n2w = sbuf("c_n2w", [128, 8], F32)
        c_mnw = sbuf("c_mnw", [128, 16], F32)
        c_dtb = sbuf("c_dtb", [NH, 1], F32)
        c_negA = sbuf("c_negA", [NH, 1], F32)
        c_negAt = sbuf("c_negAt", [128, NH], F32)
        c_b1 = sbuf("c_b1", [128, 32], F32)
        c_b2 = sbuf("c_b2", [128, 8], F32)
        c_fb1 = sbuf("c_fb1", [128, 32], F32)
        c_fb2 = sbuf("c_fb2", [128, 8], F32)
        c_gmask = sbuf("c_gmask", [8, 1], F32)
        c_umat = sbuf("c_umat", [8, 8], F32)
        t_a = sbuf("t_a", [NH, 1], F32)
        t_ar = sbuf("t_ar", [1, NH], F32)

        p_Csm = sbuf("p_Csm", [128, nchunk * 128], BF16)
        p_expl = sbuf("p_expl", [128, nchunk * NH], BF16)
        p_dab = sbuf("p_dab", [128, nchunk * NH], F32)

        p_Srun = sbuf("p_Srun", [128, DIN], F32)
        p_sumL = sbuf("p_sumL", [1, NH], F32)
        p_maxxn = sbuf("p_maxxn", [128, 1], F32)
        p_maxh = sbuf("p_maxh", [128, 1], F32)
        p_Sprop = sbuf("p_Sprop", [128, DIN], BF16)
        p_w8 = sbuf("p_w8", [8, NH], F32)
        p_w8r = sbuf("p_w8r", [1, 8 * NH], F32)
        p_wbj = sbuf("p_wbj", [128, NH], F32)
        p_sc = sbuf("p_sc", [1, 16], F32)
        p_rq1 = sbuf("p_rq1", [128, 1], F32)
        p_rq2 = sbuf("p_rq2", [128, 1], F32)
        p_fs1 = sbuf("p_fs1", [128, 1], F32)
        p_fs2 = sbuf("p_fs2", [128, 1], F32)
        p_b1s = sbuf("p_b1s", [128, 32], F32)
        p_fb2s = sbuf("p_fb2s", [128, 8], F32)
        p_stats = sbuf("p_stats", [128, 12], F32)
        p_mv = sbuf("p_mv", [128, 2], F32)
        p_rr = sbuf("p_rr", [128, 40], F32)

        AB_N = 61184
        AF_N = 11392
        ar_bf = sbuf("ar_bf", [128, AB_N], BF16)
        ar_f = sbuf("ar_f", [128, AF_N], F32)
        d2_t = sbuf("d2_t", [128, (NH // 2) * 128], F32R)

        class Alloc:
            def __init__(self):
                self.bf = 0
                self.f = 0

            def abf(self, n):
                a = self.bf
                self.bf += n
                assert self.bf <= AB_N, f"bf arena overflow {self.bf}"
                return ar_bf[:, a:a + n]

            def af(self, n):
                a = self.f
                self.f += n
                assert self.f <= AF_N, f"f32 arena overflow {self.f}"
                return ar_f[:, a:a + n]

        pb = [nc.alloc_psum_tensor(f"pb{i}", [128, 512], F32) for i in range(8)]
        pbb = [p.bitcast(BF16) for p in pb]

        emit, wait, op, dma = self.emit, self.wait, self.op, self.dma
        wtok, wq = self.wtok, self.wq
        tt, ts, stt, cp, act, mm, tr, red = (self.tt, self.ts, self.stt, self.cp,
                                             self.act, self.mm, self.tr, self.red)
        bco, bci, r3 = self.bco, self.bci, self.r3
        cnt, sems = self.cnt, self.sems

        tk = {}
        bank_u = {}
        dbg_slots = {}

        def bwait(eng, b):
            u = bank_u.get(b)
            if u is not None:
                wait(eng, u[0], u[1])

        def bset(b, semn):
            bank_u[b] = (semn, cnt[semn])

        def mkdbg(name, src_ap, shape=None, dt=None):
            if name not in self.debug_outs:
                return
            shape = list(src_ap.shape) if shape is None else shape
            dt = src_ap.dtype if dt is None else dt
            d = nc.dram_tensor("dbg_" + name, shape, dt, kind="ExternalOutput").ap()
            self.wait("gpsimd", "dv")
            self.wait("gpsimd", "ac")
            self.wait("gpsimd", "mm")
            dma("gpsimd", d, src_ap, "gld")

        A0 = Alloc()
        q_stage = A0.af(HMLP)
        q_t1 = A0.af(2048)
        q_ge = A0.af(2048)
        q_int = A0.af(2048).bitcast(I32)
        q_bf = A0.abf(HMLP)
        q_red = A0.af(2)
        q_stage2 = A0.abf(2 * HMLP).bitcast(F32)
        q_bf2 = A0.abf(HMLP)
        q_stg = [q_stage, q_stage2]
        q_bfp = [q_bf, q_bf2]

        dma("sync", c_eye[:, :], eye_f)
        dma("sync", c_triu[:, :], triu_f)
        dma("sync", c_onesr[:, :], ones_row)
        dma("sync", c_n1w[:, :], norm1w)
        dma("sync", c_n2w[:, :], norm2w)
        dma("sync", c_mnw[:, :], mnormw)
        dma("sync", c_dtb[:, :], dtb)
        dma("sync", c_b1[:, :], b1)
        dma("sync", c_b2[:, :], b2)
        dma("sync", c_fb1[:, :], fb1)
        dma("sync", c_fb2[:, :], fb2)
        dma("sync", c_gmask[:, :], gmask)
        dma("sync", c_umat[:, :], umat_t)
        dma("sync", c_convb[:, :], convb)
        dma("sync", t_a[:, :], alog)
        dma("sync", t_ar[:, :], alog_row)
        dma("gpsimd", c_eyeb[:, :], eye_f)
        dma("gpsimd", c_triub[:, :], triu_f)
        dma("gpsimd", c_ptl[:, :], ptl_f)
        dma("gpsimd", c_onescb[:, :], ones_row.rearrange("o p -> p o"))
        dma("gpsimd", c_dtile[:, :], dtile)
        dma("sync", c_convw[:, :], convw)

        wq("scalar", "sld")
        act(c_negA[:, :], t_a[:, :], AF.Exp)
        wait("vector", "ac")
        ts(c_negA[:, :], c_negA[:, :], -1.0)
        act(t_ar[:, :], t_ar[:, :], AF.Exp)
        wait("vector", "ac")
        ts(t_ar[:, :], t_ar[:, :], -1.0)
        wait("tensor", "dv")
        wq("tensor", "gld")
        mm(pb[0][:, 0:NH], c_onesr[0:1, :], t_ar[0:1, 0:NH], True, True)
        wait("vector", "mm")
        cp(c_negAt[:, :], pb[0][:, 0:NH])

        op("vector", lambda E: E.memset(p_Srun[:, :], 0.0), "dv")
        op("vector", lambda E: E.memset(p_maxxn[:, :], 0.0), "dv")
        op("vector", lambda E: E.memset(p_maxh[:, :], 0.0), "dv")
        op("vector", lambda E: E.memset(p_sumL[:, :], 0.0), "dv")

        def phase_barrier():
            for e in ENGS:
                for cs in ("mm", "dv", "ac"):
                    wait(e, cs)
                wq(e, "sld")
                wq(e, "gld")
                wq(e, "ost")

        def bcast_col(dst, src_row):
            n = src_row.shape[1]
            wait("tensor", "dv")
            bwait("tensor", 6)
            mm(pb[6][:, 0:n], c_onesr[0:1, :], src_row, True, True)
            mv = cnt["mm"]
            wait("vector", "mm", mv)
            cp(dst, pb[6][:, 0:n])
            bset(6, "dv")

        def quant_round(dst_bf, src, w, iscr, fscr):
            cp(iscr[:, 0:w], src)
            cp(fscr[:, 0:w], iscr[:, 0:w])
            tt(fscr[:, 0:w], src, fscr[:, 0:w], OP.subtract)
            ts(dst_bf, fscr[:, 0:w], -0.5, -1.0, OP.is_le, OP.mult)
            stt(dst_bf, fscr[:, 0:w], 0.5, dst_bf, OP.is_ge, OP.add)
            cp(fscr[:, 0:w], iscr[:, 0:w])
            tt(dst_bf, fscr[:, 0:w], dst_bf, OP.add)

        q_lc = {}
        q_st = {}

        def qload(i, width, wsrc, roff):
            b = q_stg[i % 2]
            if q_lc.get(i % 2) is None:
                wait("sync", "dv")
            else:
                wait("sync", "dv", q_lc[i % 2])
            v = dma("sync", b[:, 0:width], wsrc[roff * 128:(roff + 1) * 128, :])
            wtok("vector", v)
            return b

        def qstore_guard(i):
            if q_st.get(i % 2) is None:
                wq("vector", "gld")
            else:
                wtok("vector", q_st[i % 2])

        def wquant(wsrc, nchunks, width, col, scratch, dc_major=False):
            HW = 2048
            for c in range(nchunks):
                b = qload(c, width, wsrc, c)
                red(q_red[:, 0:1], b[:, 0:width])
                if c == 0:
                    cp(q_red[:, 1:2], q_red[:, 0:1])
                else:
                    tt(q_red[:, 1:2], q_red[:, 1:2], q_red[:, 0:1], OP.max)
                q_lc[c % 2] = cnt["dv"]
            wait("tensor", "dv")
            tr(pb[1][0:1, 0:128], q_red[:, 1:2], c_eye[:, :])
            wait("vector", "mm")
            red(p_sc[0:1, 15:16], pb[1][0:1, 0:128], OP.max, absv=False)
            ts(p_sc[0:1, col:col + 1], p_sc[0:1, 15:16], 1e-6, 1.0 / 127.0, OP.max, OP.mult)
            op("vector", lambda E: E.reciprocal(p_sc[0:1, col + 1:col + 2], p_sc[0:1, col:col + 1]), "dv")
            bcast_col(q_red[:, 0:1], p_sc[0:1, col + 1:col + 2])
            for c in range(nchunks):
                b = qload(c, width, wsrc, c)
                qb = q_bfp[c % 2]
                qstore_guard(c)
                for h0 in range(0, width, HW):
                    w2 = min(HW, width - h0)
                    ts(q_t1[:, 0:w2], b[:, h0:h0 + w2], q_red[:, 0:1])
                    quant_round(qb[:, h0:h0 + w2], q_t1[:, 0:w2], w2, q_int, q_ge)
                q_lc[c % 2] = cnt["dv"]
                dv = cnt["dv"]
                wait("gpsimd", "dv", dv)
                if dc_major:
                    q_st[c % 2] = dma("gpsimd",
                        scratch[:, :, c * 128:(c + 1) * 128].rearrange("d p c -> p d c"),
                        qb[:, 0:width].rearrange("p (d c) -> p d c", d=8), "gld")
                else:
                    q_st[c % 2] = dma("gpsimd", scratch[:, c * width:(c + 1) * width],
                                      qb[:, 0:width], "gld")

        wquant(fw1_t, 8, HMLP, 0, sc_w1q)
        wquant(fw2_t, 32, D, 2, sc_w2q, dc_major=True)
        for k in range(32):
            b = qload(k, D, w2_t, k)
            qb = q_bfp[k % 2]
            qstore_guard(k)
            cp(qb[:, 0:D], b[:, 0:D])
            q_lc[k % 2] = cnt["dv"]
            dv = cnt["dv"]
            wait("gpsimd", "dv", dv)
            q_st[k % 2] = dma("gpsimd",
                sc_w2f[:, :, k * 128:(k + 1) * 128].rearrange("d p c -> p d c"),
                qb[:, 0:D].rearrange("p (d c) -> p d c", d=8), "gld")
        for k in range(8):
            b = qload(k, HMLP, w1_t, k)
            qb = q_bfp[k % 2]
            qstore_guard(k)
            ts(qb[:, 0:HMLP], b[:, 0:HMLP], c_n2w[:, k:k + 1])
            q_lc[k % 2] = cnt["dv"]
            dv = cnt["dv"]
            wait("gpsimd", "dv", dv)
            q_st[k % 2] = dma("gpsimd", sc_w1f[:, k * HMLP:(k + 1) * HMLP],
                              qb[:, 0:HMLP], "gld")

        phase_barrier()
        A = Alloc()
        a_x = A.af(4 * D)
        a_xn = A.af(D)
        a_l = A.af(slab)
        a_dl = A.af(slab)
        a_dt = A.af(slab)
        a_ltm = A.af(4 * NH)
        a_deltm = A.af(4 * NH)
        a_lendb = A.af(4 * NH)
        a_bn1 = A.abf(D)
        a_bn1fm = A.abf(8 * slab)
        a_wv = A.abf(2 * 8 * 512)
        a_wdt = A.abf(8 * NH)
        a_xbc = A.abf(NCC * (slab + 3))
        a_xbco = A.abf(NCC * slab)
        a_zsf = A.abf(16 * slab)
        a_cu = A.abf(1024)
        a_ypre = A.abf(DIN)
        a_xhtm = A.abf(2 * DIN)
        a_xw = A.abf(2 * DIN)
        a_xw2 = A.abf(2 * DIN)
        a_dxh = a_xw2
        a_btm = A.abf(2 * 128)
        a_wp = A.abf((NH // 2) * 128)
        a_np = None
        a_gc = A.abf(2 * 128)
        a_srb = A.abf(DIN)
        a_dtmb = A.abf(4 * NH)
        a_wcol = A.abf(4 * NH)
        a_y1 = a_xw
        a_np = a_zsf
        a_lrow = A.af(4 * NH)
        a_xnf = A.af(8 * 128)
        assert A.bf <= AB_N and A.f <= AF_N

        def rsqrt_newton(dst, src, scr0, scr1):
            vd = cnt["dv"]
            wait("scalar", "dv", vd)
            act(scr0, src, AF.Sqrt)
            wait("vector", "ac")
            op("vector", lambda E: E.reciprocal(scr1, scr0), "dv")
            tt(scr0, scr1, scr1)
            tt(scr0, scr0, src)
            ts(scr0, scr0, -0.5, 1.5, OP.mult, OP.add)
            tt(dst, scr1, scr0)

        def subtile(row, i, buf, pre, tok=None, st=None):
            xap = a_x[:, buf * D:(buf + 1) * D]
            if tok is None:
                wait("sync", "dv")
                wait("sync", "ac")
                tok = dma("sync", xap, xin[row:row + 128, :])
            if st is None:
                wtok("vector", tok)
                stt(a_xn[:, :], xap, 1.0, xap, OP.mult, OP.bypass, accum=p_stats[:, 0:1])
                stt(a_xn[:, :], xap, 1.0, xap, OP.mult, OP.mult, accum=p_stats[:, 1:2])
                m = p_mv[:, 0:1]
                var = p_mv[:, 1:2]
                ts(m, p_stats[:, 0:1], 1.0 / D)
                ts(p_stats[:, 2:3], p_stats[:, 1:2], 1.0 / D)
                tt(p_stats[:, 3:4], m, m)
                tt(var, p_stats[:, 2:3], p_stats[:, 3:4], OP.subtract)
                vpe = p_rr[:, 0:1]
                rstd = p_rr[:, 1:2]
                r2 = p_rr[:, 2:3]
                sb_ = p_rr[:, 3:4]
                sc2 = p_rr[:, 4:5]
                sb2 = p_rr[:, 5:6]
                s0 = p_rr[:, 6:7]
                s1 = p_rr[:, 7:8]
                ts(vpe, var, 1e-5, o1=OP.add)
                rsqrt_newton(rstd, vpe, s0, s1)
                tt(r2, rstd, rstd)
                tt(r2, r2, var)
                ts(r2, r2, 1024.0, 1e-6, OP.mult, OP.add)
                rsqrt_newton(r2, r2, s0, s1)
                tt(sb_, m, rstd)
                ts(sb_, sb_, -1.0)
                tt(sc2, rstd, r2)
                tt(sb2, sb_, r2)
            else:
                rstd, sb_, sc2, sb2 = st
            vd = cnt["dv"]
            wait("scalar", "dv", vd)
            if not pre:
                act(a_xn[:, :], xap, AF.Identity, scale=rstd, bias=sb_)
            act(a_bn1[:, :], xap, AF.Identity, scale=sc2, bias=sb2)
            acd = cnt["ac"]
            tk[("xc", buf)] = ("ac", acd)
            if not pre:
                wait("vector", "ac", acd)
                red(p_rr[:, 8:9], a_xn[:, :])
                tt(p_maxxn[:, :], p_maxxn[:, :], p_rr[:, 8:9], OP.max)
            for half in range(2):
                wait("tensor", "ac", acd)
                wait("tensor", "dv")
                for j in range(4):
                    c = half * 4 + j
                    tr(pbb[j][:, 0:128], a_bn1[:, c * 128:(c + 1) * 128], c_eyeb[:, :])
                    if not pre:
                        tr(pb[4 + j][:, 0:128], a_xn[:, c * 128:(c + 1) * 128], c_eye[:, :])
                mv = cnt["mm"]
                wait("vector", "mm", mv)
                if half == 0 and i == 0:
                    wtok("vector", tk.get("bn1fm"))
                for j in range(4):
                    c = half * 4 + j
                    ts(a_bn1fm[:, c * slab + i * 128:c * slab + (i + 1) * 128],
                       pbb[j][:, 0:128], c_n1w[:, c:c + 1])
                    bset(j, "dv")
                if not pre:
                    if half == 0:
                        wtok("vector", tk.get("xnf"))
                    for j in range(4):
                        c = half * 4 + j
                        cp(a_xnf[:, c * 128:(c + 1) * 128], pb[4 + j][:, 0:128])
                        bset(4 + j, "dv")
            if not pre:
                s_idx = (row - 128) // slab
                dv = cnt["dv"]
                wait("gpsimd", "dv", dv)
                tk["xnf"] = dma("gpsimd",
                    st_xnfm[s_idx].rearrange("p (c t) -> p c t", c=8)[:, :, i * 128:(i + 1) * 128],
                    a_xnf.rearrange("p (c t) -> p c t", c=8), "gld")

        def inproj_waves(width, pre):
            waves = []
            if pre != "z":
                waves += [("x", w) for w in range(0, NCC, 4)]
            if pre == "z":
                waves += [("z", w) for w in range(0, 16, 4)]
            for wv_i, (kind, w0) in enumerate(waves):
                par = wv_i % 2
                bank0 = par * 4
                nchk = min(4, (16 if kind == "z" else NCC) - w0)
                colbase = (0 if kind == "z" else DIN) + w0 * 128
                ncols = nchk * 128
                buf = a_wv[:, par * 4096:par * 4096 + 8 * ncols]
                wtok("gpsimd", tk.get(("wv", par)))
                v = dma("gpsimd", buf.rearrange("p (k c) -> p k c", k=8),
                        win_t[:, colbase:colbase + ncols].rearrange("(k p) c -> p k c", p=128), "gld")
                last_wave = kind == "x" and w0 + 4 >= NCC and pre is False
                if last_wave:
                    vdt = dma("gpsimd", a_wdt[:, 0:8 * NH].rearrange("p (k c) -> p k c", k=8),
                              win_t[:, DIN + CONVD:DIP].rearrange("(k p) c -> p k c", p=128), "gld")
                wtok("tensor", v)
                if last_wave:
                    wtok("tensor", vdt)
                for j in range(nchk + (1 if last_wave else 0)):
                    bwait("tensor", bank0 + j)
                wait("tensor", "dv")
                for k in range(8):
                    for j in range(nchk):
                        mm(pb[bank0 + j][:, 0:width],
                           buf[:, k * ncols + j * 128:k * ncols + (j + 1) * 128],
                           a_bn1fm[:, k * slab:k * slab + width],
                           start=(k == 0), stop=(k == 7))
                    if last_wave:
                        mm(pb[bank0 + nchk][0:NH, 0:width],
                           a_wdt[:, k * NH:(k + 1) * NH],
                           a_bn1fm[:, k * slab:k * slab + width],
                           start=(k == 0), stop=(k == 7))
                mv = cnt["mm"]
                tk[("wv", par)] = ("mm", mv)
                wait("scalar", "mm", mv)
                if wv_i == 0:
                    wait("scalar", "dv")
                    wait("scalar", "mm")
                for j in range(nchk):
                    c = w0 + j
                    if kind == "z":
                        act(a_zsf[:, c * slab:c * slab + width], pb[bank0 + j][:, 0:width], AF.Silu)
                    else:
                        act(a_xbc[:, c * (slab + 3) + 3:c * (slab + 3) + 3 + width],
                            pb[bank0 + j][:, 0:width], AF.Copy)
                    bset(bank0 + j, "ac")
                if last_wave:
                    act(a_dt[0:NH, 0:width], pb[bank0 + nchk][0:NH, 0:width], AF.Exp, bias=c_dtb[:, :])
                    bset(bank0 + nchk, "ac")
                    act(a_dt[0:NH, 0:width], a_dt[0:NH, 0:width], AF.Ln, bias=1.0)

        def conv_slab(width):
            wait("scalar", "mm")
            wait("vector", "ac")
            for c in range(NCC):
                base = c * (slab + 3)
                ob = c * slab
                w = c_convw
                cu = a_cu[:, (c % 2) * 512:(c % 2) * 512 + width]
                if c >= 2:
                    wait("vector", "ac")
                ts(cu, a_xbc[:, base + 0:base + width], w[:, c * 4 + 0:c * 4 + 1])
                stt(cu, a_xbc[:, base + 1:base + 1 + width],
                    w[:, c * 4 + 1:c * 4 + 2], cu, OP.mult, OP.add)
                stt(cu, a_xbc[:, base + 2:base + 2 + width],
                    w[:, c * 4 + 2:c * 4 + 3], cu, OP.mult, OP.add)
                stt(cu, a_xbc[:, base + 3:base + 3 + width],
                    w[:, c * 4 + 3:c * 4 + 4], cu, OP.mult, OP.add)
                dv = cnt["dv"]
                wait("scalar", "dv", dv)
                act(a_xbco[:, ob:ob + width], cu, AF.Silu, bias=c_convb[:, c:c + 1])
            tk["xbco"] = ("ac", cnt["ac"])
            wait("vector", "ac")
            op("vector", lambda E: E.tensor_copy(
                r3(a_xbc, NCC, slab + 3)[:, :, 0:3],
                r3(a_xbc, NCC, slab + 3)[:, :, width:width + 3]), "dv")

        def scan_chunk(s, c):
            g = s * ncs + c
            t0 = c * Q
            lsl = slice(c * NH, (c + 1) * NH)
            par = g % 2
            xh_ = a_xhtm[:, par * DIN:(par + 1) * DIN]
            xw_ = a_xw[:, par * DIN:(par + 1) * DIN]
            xw2_ = a_xw2[:, par * DIN:(par + 1) * DIN]
            dxh_ = xw2_
            y1_ = xw_
            btm_ = a_btm[:, par * 128:(par + 1) * 128]
            gc_ = a_gc[:, par * 128:(par + 1) * 128]
            np_ = a_np[:, par * NH * 128:(par + 1) * NH * 128]
            srcs = [(a_xbco[:, cc * slab + t0:cc * slab + t0 + Q], xh_[:, cc * 128:(cc + 1) * 128])
                    for cc in range(16)]
            srcs.append((a_xbco[:, 16 * slab + t0:16 * slab + t0 + Q], btm_))
            for w0 in range(0, len(srcs), 4):
                grp = srcs[w0:w0 + 4]
                for j in range(len(grp)):
                    bwait("tensor", 4 + j)
                if w0 == 0:
                    wtok("tensor", tk.get(("scan_sb", par)))
                    wtok("tensor", tk.get("xbco"))
                for j, (srcap, dstap) in enumerate(grp):
                    tr(pbb[4 + j][:, 0:128], srcap, c_eyeb[:, :])
                mv = cnt["mm"]
                wait("vector", "mm", mv)
                for j, (srcap, dstap) in enumerate(grp):
                    cp(dstap, pbb[4 + j][:, 0:128])
                    bset(4 + j, "dv")
            cp(p_Csm[:, g * 128:(g + 1) * 128], a_xbco[:, 17 * slab + t0:17 * slab + t0 + Q])
            wait("vector", "ac")
            tt(xw_, xh_, bco(a_dtmb[:, lsl], HD))
            tt(xw2_, xw_, bco(a_wcol[:, lsl], HD))
            bwait("tensor", 6)
            mm(pb[6][:, 0:128], a_xbco[:, 16 * slab + t0:16 * slab + t0 + Q],
               a_xbco[:, 17 * slab + t0:17 * slab + t0 + Q], True, True)
            mv = cnt["mm"]
            wait("vector", "mm", mv)
            tt(gc_, pb[6][:, 0:128], c_triub[:, :])
            bset(6, "dv")
            for j in range(4):
                bwait("tensor", j)
            dv = cnt["dv"]
            wait("tensor", "dv", dv)
            for j in range(4):
                mm(pb[j][:, 0:512], btm_, xw2_[:, j * 512:(j + 1) * 512], True, True)
            mvS = cnt["mm"]
            wtok("vector", tk.get("srb"))
            cp(a_srb[:, :], p_Srun[:, :])
            tt(r3(p_Srun, NH, HD), r3(p_Srun, NH, HD), bco(p_dab[:, g * NH:(g + 1) * NH], HD))
            wait("vector", "mm", mvS)
            for j in range(4):
                tt(p_Srun[:, j * 512:(j + 1) * 512], p_Srun[:, j * 512:(j + 1) * 512],
                   pb[j][:, 0:512], OP.add)
                bset(j, "dv")
            tt(dxh_, xh_, c_dtile[:, :])
            wtok("vector", tk.get("zs"))
            for half in range(2):
                wtok("vector", tk.get("d2"))
                tt(r3(d2_t, NH // 2, 128),
                   bco(a_deltm[:, c * NH + half * 16:c * NH + (half + 1) * 16], 128),
                   bci(c_triu[:, :], NH // 2))
                for j in range(4):
                    bwait("tensor", 4 + j)
                wait("tensor", "dv")
                for j in range(4):
                    mm(pb[4 + j][:, 0:512], c_ptl[:, :],
                       d2_t[:, j * 512:(j + 1) * 512], True, True)
                mv = cnt["mm"]
                tk["d2"] = ("mm", mv)
                wtok("scalar", tk.get("wp"))
                wait("scalar", "mm", mv)
                for j in range(4):
                    act(a_wp[:, j * 512:(j + 1) * 512], pb[4 + j][:, 0:512], AF.Exp)
                    bset(4 + j, "ac")
                acv = cnt["ac"]
                wait("vector", "ac", acv)
                tt(r3(np_[:, half * 2048:(half + 1) * 2048], NH // 2, 128),
                   r3(a_wp[:, 0:2048], NH // 2, 128), bci(gc_, NH // 2))
                tk["wp"] = ("dv", cnt["dv"])
            dv = cnt["dv"]
            for j in range(4):
                bwait("tensor", j)
            wait("tensor", "dv", dv)
            for j in range(4):
                mm(pb[j][:, 0:512], c_eyeb[:, :], dxh_[:, j * 512:(j + 1) * 512], True, False)
            for h in range(NH):
                j = h // 8
                mm(pb[j][:, (h % 8) * 64:(h % 8) * 64 + 64],
                   np_[:, h * 128:(h + 1) * 128], xw_[:, h * 64:(h + 1) * 64],
                   False, h % 8 == 7)
            for j in range(4):
                bwait("tensor", 4 + j)
            for j in range(4):
                mm(pb[4 + j][:, 0:512], p_Csm[:, g * 128:(g + 1) * 128],
                   a_srb[:, j * 512:(j + 1) * 512], True, True)
            mv = cnt["mm"]
            tk["srb"] = ("mm", mv)
            wait("vector", "mm", mv)
            for j in range(4):
                tt(y1_[:, j * 512:(j + 1) * 512].rearrange("p (h e) -> p h e", h=8),
                   r3(pb[4 + j][:, 0:512], 8, 64),
                   bco(p_expl[:, g * NH + j * 8:g * NH + (j + 1) * 8], HD))
                bset(4 + j, "dv")
            wtok("vector", tk.get("ypre"))
            for j in range(4):
                tt(a_ypre[:, j * 512:(j + 1) * 512], y1_[:, j * 512:(j + 1) * 512],
                   pb[j][:, 0:512], OP.add)
                bset(j, "dv")
            dv = cnt["dv"]
            wait("gpsimd", "dv", dv)
            tk["ypre"] = dma("gpsimd", st_ypre[g, :, :], a_ypre[:, :], "gld")
            tk[("scan_sb", par)] = ("dv", dv)

        def subtiles(s):
            nsub = slab // 128
            toks = []
            for i in range(nsub):
                wtok("sync", tk.get(("xc", i)))
                toks.append(dma("sync", a_x[:, i * D:(i + 1) * D],
                                xin[128 + s * slab + i * 128:128 + s * slab + (i + 1) * 128, :]))
            for i in range(nsub):
                xap = a_x[:, i * D:(i + 1) * D]
                wtok("vector", toks[i])
                stt(a_xn[:, :], xap, 1.0, xap, OP.mult, OP.bypass, accum=p_stats[:, i:i + 1])
                stt(a_xn[:, :], xap, 1.0, xap, OP.mult, OP.mult, accum=p_stats[:, 4 + i:5 + i])
            m4 = p_rr[:, 0:nsub]
            v4 = p_rr[:, 4:4 + nsub]
            rstd = p_rr[:, 8:8 + nsub]
            r2 = p_rr[:, 12:12 + nsub]
            sb_ = p_rr[:, 16:16 + nsub]
            sc2 = p_rr[:, 20:20 + nsub]
            sb2 = p_rr[:, 24:24 + nsub]
            s0 = p_rr[:, 28:28 + nsub]
            s1 = p_rr[:, 32:32 + nsub]
            ts(m4, p_stats[:, 0:nsub], 1.0 / D)
            ts(v4, p_stats[:, 4:4 + nsub], 1.0 / D)
            tt(s0, m4, m4)
            tt(v4, v4, s0, OP.subtract)
            ts(s0, v4, 1e-5, o1=OP.add)
            rsqrt_newton(rstd, s0, s1, p_rr[:, 36:36 + nsub])
            tt(r2, rstd, rstd)
            tt(r2, r2, v4)
            ts(r2, r2, 1024.0, 1e-6, OP.mult, OP.add)
            rsqrt_newton(r2, r2, s1, p_rr[:, 36:36 + nsub])
            tt(sb_, m4, rstd)
            ts(sb_, sb_, -1.0)
            tt(sc2, rstd, r2)
            tt(sb2, sb_, r2)
            for i in range(nsub):
                subtile(128 + s * slab + i * 128, i, i, pre=False, tok=toks[i],
                        st=(rstd[:, i:i + 1], sb_[:, i:i + 1], sc2[:, i:i + 1], sb2[:, i:i + 1]))

        def delta_l(s):
            wait("vector", "ac")
            wait("vector", "mm")
            ts(a_dl[0:NH, 0:slab], a_dt[0:NH, 0:slab], c_negA[:, :])
            for c in range(ncs):
                op("vector", lambda E, c=c: E.tensor_tensor_scan(
                    out=a_l[0:NH, c * Q:(c + 1) * Q], data0=a_dl[0:NH, c * Q:(c + 1) * Q],
                    data1=a_dl[0:NH, c * Q:(c + 1) * Q], initial=0.0,
                    op0=OP.add, op1=OP.bypass), "dv")
            for c in range(ncs):
                bwait("tensor", c)
                bwait("tensor", 4 + c)
            wait("tensor", "dv")
            for c in range(ncs):
                tr(pb[c][0:128, 0:NH], a_l[0:NH, c * Q:(c + 1) * Q], c_eye[0:NH, 0:NH])
                tr(pb[4 + c][0:128, 0:NH], a_dt[0:NH, c * Q:(c + 1) * Q], c_eye[0:NH, 0:NH])
            mv = cnt["mm"]
            wait("vector", "mm", mv)
            for c in range(ncs):
                cp(a_ltm[:, c * NH:(c + 1) * NH], pb[c][:, 0:NH])
                cp(a_dtmb[:, c * NH:(c + 1) * NH], pb[4 + c][:, 0:NH])
                bset(c, "dv")
                tt(a_deltm[:, c * NH:(c + 1) * NH], pb[4 + c][:, 0:NH], c_negAt[:, :])
                bset(4 + c, "dv")
            dv = cnt["dv"]
            wait("scalar", "dv", dv)
            act(p_expl[:, s * ncs * NH:(s + 1) * ncs * NH], a_ltm[:, 0:ncs * NH], AF.Exp)
            wait("sync", "dv", dv)
            vr = dma("sync", a_lrow[0:1, 0:ncs * NH], a_ltm[127:128, 0:ncs * NH])
            wtok("vector", vr)
            for c in range(ncs):
                tt(p_sumL[0:1, :], p_sumL[0:1, :], a_lrow[0:1, c * NH:(c + 1) * NH], OP.add)
            wtok("tensor", vr)
            bwait("tensor", 6)
            mm(pb[6][:, 0:ncs * NH], c_onesr[0:1, :], a_lrow[0:1, 0:ncs * NH], True, True)
            mv = cnt["mm"]
            wait("vector", "mm", mv)
            cp(a_lendb[:, 0:ncs * NH], pb[6][:, 0:ncs * NH])
            wait("scalar", "mm", mv)
            act(p_dab[:, s * ncs * NH:(s + 1) * ncs * NH], pb[6][:, 0:ncs * NH], AF.Exp)
            bset(6, "ac")
            tt(a_lendb[:, 0:ncs * NH], a_lendb[:, 0:ncs * NH], a_ltm[:, 0:ncs * NH], OP.subtract)
            dv = cnt["dv"]
            wait("scalar", "dv", dv)
            act(a_wcol[:, 0:ncs * NH], a_lendb[:, 0:ncs * NH], AF.Exp)

        def zs_stash(s):
            wait("gpsimd", "ac")
            tk["zs"] = dma("gpsimd", st_zs[s, :, :], a_zsf[:, :], "gld")

        subtile(0, 0, 0, pre=True)
        inproj_waves(128, pre=True)
        wait("vector", "ac")
        op("vector", lambda E: E.tensor_copy(
            a_xbc.rearrange("p (c w) -> p c w", c=NCC)[:, :, 0:3],
            a_xbc.rearrange("p (c w) -> p c w", c=NCC)[:, :, 128:131]), "dv")
        subtiles(0)
        inproj_waves(slab, pre=False)
        inproj_waves(slab, pre="z")
        tk["bn1fm"] = ("mm", cnt["mm"])
        delta_l(0)
        conv_slab(slab)
        zs_stash(0)
        for s in range(nslab):
            nxt = s + 1 < nslab
            scan_chunk(s, 0)
            if ncs > 1:
                scan_chunk(s, 1)
            if nxt:
                subtiles(s + 1)
            for c in range(2, ncs):
                scan_chunk(s, c)
            if nxt:
                inproj_waves(slab, pre=False)
                inproj_waves(slab, pre="z")
                tk["bn1fm"] = ("mm", cnt["mm"])
                delta_l(s + 1)
                conv_slab(slab)
                zs_stash(s + 1)

        phase_barrier()
        wait("tensor", "dv")
        bwait("tensor", 0)
        tr(pb[0][0:1, 0:128], p_maxxn[:, :], c_eye[:, :])
        mv = cnt["mm"]
        wait("vector", "mm", mv)
        red(p_sc[0:1, 13:14], pb[0][0:1, 0:128], OP.max, absv=False)
        bset(0, "dv")
        dv = cnt["dv"]
        wait("gpsimd", "dv", dv)
        zw = CC1W - DIN
        zbuf = ar_f[:, 0:zw]
        op("vector", lambda E: E.memset(zbuf, 0.0), "dv")
        dv = cnt["dv"]
        wait("gpsimd", "dv", dv)
        zt = dma("gpsimd", cc1_in[:, DIN:CC1W], zbuf, "gld")
        dma("gpsimd", cc1_in[:, 0:DIN], p_Srun[:, :], "gld")
        wtok("gpsimd", zt)
        dma("gpsimd", cc1_in[0:1, DIN:DIN + NH], p_sumL[0:1, :], "gld")
        dma("gpsimd", cc1_in[0:1, DIN + NH:DIN + NH + 1], p_sc[0:1, 13:14], "gld")
        wq("gpsimd", "gld")
        op("gpsimd", lambda E: E.collective_compute(
            "AllGather", OP.bypass, replica_groups=[list(range(8))],
            ins=[cc1_in.opt()], outs=[cc1_out.opt()]), "cc", 1)
        wait("gpsimd", "cc", cnt["cc"])
        t1_ = dma("gpsimd", p_w8[0:8, 0:NH], cc1_out[:, 0, DIN:DIN + NH], "gld")
        t2_ = dma("gpsimd", p_w8r[0:1, 0:8], cc1_out[:, 0, DIN + NH:DIN + NH + 1], "gld", slow=True)
        wtok("vector", t1_)
        wtok("vector", t2_)
        red(p_sc[0:1, 4:5], p_w8r[0:1, 0:8], OP.max, absv=False)
        ts(p_sc[0:1, 5:6], p_sc[0:1, 4:5], 1e-6, 1.0 / 127.0, OP.max, OP.mult)
        op("vector", lambda E: E.reciprocal(p_sc[0:1, 6:7], p_sc[0:1, 5:6]), "dv")
        tt(p_sc[0:1, 10:11], p_sc[0:1, 0:1], p_sc[0:1, 5:6])
        bcast_col(p_rq1[:, 0:1], p_sc[0:1, 6:7])
        bcast_col(p_fs1[:, 0:1], p_sc[0:1, 10:11])
        ts(p_b1s[:, :], c_fb1[:, :], p_fs1[:, 0:1])
        wtok("tensor", t1_)
        bwait("tensor", 1)
        mm(pb[1][0:8, 0:NH], c_umat[0:8, 0:8], p_w8[0:8, 0:NH], True, True)
        mv = cnt["mm"]
        wait("scalar", "mm", mv)
        act(p_w8[0:8, 0:NH], pb[1][0:8, 0:NH], AF.Exp)
        bset(1, "ac")
        wait("vector", "ac")
        ts(p_w8[0:8, 0:NH], p_w8[0:8, 0:NH], c_gmask[0:8, 0:1])
        dv = cnt["dv"]
        wait("sync", "dv", dv)
        v = dma("sync", p_w8r[0:1, 0:8 * NH], p_w8[0:8, 0:NH])
        CM = Alloc()
        p_stmp = CM.af(DIN)
        p_Sin = CM.af(DIN)
        op("vector", lambda E: E.memset(p_Sin[:, :], 0.0), "dv")
        wq("sync", "gld")
        wtok("tensor", v)
        for j in range(8):
            bwait("tensor", 2)
            mm(pb[2][:, 0:NH], c_onesr[0:1, :], p_w8r[0:1, j * NH:(j + 1) * NH], True, True)
            mv = cnt["mm"]
            v = dma("sync", p_stmp[:, 0:DIN], cc1_out[j, :, 0:DIN])
            wait("vector", "mm", mv)
            cp(p_wbj[:, :], pb[2][:, 0:NH])
            bset(2, "dv")
            wtok("vector", v)
            tt(r3(p_stmp, NH, HD), r3(p_stmp, NH, HD), bco(p_wbj[:, :], HD))
            tt(p_Sin[:, :], p_Sin[:, :], p_stmp[:, :], OP.add)
            wait("sync", "dv")
            wait("tensor", "dv")
        cp(p_Sprop[:, :], p_Sin[:, :])

        phase_barrier()
        B1 = Alloc()
        b_wout = B1.abf(16 * D)
        b_ypre4 = B1.abf(ncs * DIN)
        b_zsf = B1.abf(16 * slab)
        b_zs = B1.abf(DIN)
        b_y1 = B1.abf(DIN)
        b_yg = B1.abf(DIN)
        b_yn = B1.abf(DIN)
        b_sq = b_yn
        b_yfm = B1.abf(16 * slab)
        b_xattnb = B1.abf(8 * slab)
        b_xsq = b_yfm[:, 0:8 * slab]
        b_hin = B1.abf(8 * slab)
        b_wstage = B1.af(D)
        b_xnfm = B1.af(8 * slab)
        b_xattn = B1.af(8 * slab)
        b_ssq = B1.af(4)
        p_row = B1.af(4 * slab)[0:1, :]
        assert B1.bf <= AB_N and B1.f <= AF_N, (B1.bf, B1.f)

        for c in range(16):
            wait("sync", "dv")
            v = dma("sync", b_wstage[:, 0:D], wout_t[c * 128:(c + 1) * 128, :])
            wtok("vector", v)
            ts(b_wout[:, c * D:(c + 1) * D], b_wstage[:, 0:D], c_mnw[:, c:c + 1])

        b1_prev = None
        for s in range(nslab):
            if b1_prev is None:
                wait("sync", "dv")
            else:
                zsf_mm_p, ypre_dv_p, xnfm_dv_p = b1_prev
                wait("sync", "mm", zsf_mm_p)
                wait("sync", "dv", ypre_dv_p)
            vzf = dma("sync", b_zsf[:, :], st_zs[s, :, :])
            vy = dma("sync", b_ypre4[:, :].rearrange("p (c t) -> p c t", c=ncs),
                     st_ypre[s * ncs:(s + 1) * ncs].rearrange("c p t -> p c t"))
            if b1_prev is not None:
                wait("sync", "dv", xnfm_dv_p)
            v = dma("sync", b_xnfm[:, :], st_xnfm[s, :, :])
            for c in range(ncs):
                g = s * ncs + c
                b_ypre = b_ypre4[:, c * DIN:(c + 1) * DIN]
                wtok("tensor", vzf)
                for w0 in range(0, 16, 4):
                    for j in range(4):
                        bwait("tensor", 4 + j)
                    wait("tensor", "dv")
                    for j in range(4):
                        tr(pbb[4 + j][:, 0:128],
                           b_zsf[:, (w0 + j) * slab + c * 128:(w0 + j) * slab + (c + 1) * 128],
                           c_eyeb[:, :])
                    mv = cnt["mm"]
                    wait("vector", "mm", mv)
                    for j in range(4):
                        cp(b_zs[:, (w0 + j) * 128:(w0 + j + 1) * 128], pbb[4 + j][:, 0:128])
                        bset(4 + j, "dv")
                wait("tensor", "dv")
                for j in range(4):
                    bwait("tensor", j)
                for j in range(4):
                    mm(pb[j][:, 0:512], p_Csm[:, g * 128:(g + 1) * 128],
                       p_Sprop[:, j * 512:(j + 1) * 512], True, True)
                mv = cnt["mm"]
                wait("vector", "mm", mv)
                wtok("vector", vy)
                for j in range(4):
                    tt(b_y1[:, j * 512:(j + 1) * 512].rearrange("p (h e) -> p h e", h=8),
                       r3(pb[j][:, 0:512], 8, 64),
                       bco(p_expl[:, g * NH + j * 8:g * NH + (j + 1) * 8], HD))
                    bset(j, "dv")
                tt(b_y1[:, :], b_y1[:, :], b_ypre[:, :], OP.add)
                tt(b_yg[:, :], b_y1[:, :], b_zs[:, :])
                stt(b_sq[:, :], b_yg[:, :], 1.0, b_yg[:, :], OP.mult, OP.mult,
                    accum=b_ssq[:, 0:1])
                tt(r3(p_Sprop, NH, HD), r3(p_Sprop, NH, HD),
                   bco(p_dab[:, g * NH:(g + 1) * NH], HD))
                ts(b_ssq[:, 1:2], b_ssq[:, 0:1], 1.0 / DIN, 1e-5, OP.mult, OP.add)
                rsqrt_newton(b_ssq[:, 2:3], b_ssq[:, 1:2], p_rr[:, 9:10], p_rr[:, 10:11])
                ts(b_yn[:, :], b_yg[:, :], b_ssq[:, 2:3])
                for w0 in range(0, 16, 4):
                    for j in range(4):
                        bwait("tensor", 4 + j)
                    wait("tensor", "dv")
                    for j in range(4):
                        tr(pbb[4 + j][:, 0:128], b_yn[:, (w0 + j) * 128:(w0 + j + 1) * 128],
                           c_eyeb[:, :])
                    mv = cnt["mm"]
                    wait("vector", "mm", mv)
                    for j in range(4):
                        cp(b_yfm[:, (w0 + j) * slab + c * 128:(w0 + j) * slab + (c + 1) * 128],
                           pbb[4 + j][:, 0:128])
                        bset(4 + j, "dv")
            zsf_mm_s = cnt["mm"]
            ypre_dv_s = cnt["dv"]
            wait("tensor", "dv")
            for j in range(8):
                bwait("tensor", j)
            for k in range(16):
                for dc in range(8):
                    mm(pb[dc][:, 0:slab],
                       b_wout[:, k * D + dc * 128:k * D + (dc + 1) * 128],
                       b_yfm[:, k * slab:(k + 1) * slab],
                       start=(k == 0), stop=(k == 15))
            mv = cnt["mm"]
            wait("vector", "mm", mv)
            wtok("vector", v)
            wq("vector", "gld")
            for dc in range(8):
                tt(b_xattn[:, dc * slab:(dc + 1) * slab], pb[dc][:, 0:slab],
                   b_xnfm[:, dc * slab:(dc + 1) * slab], OP.add)
                bset(dc, "dv")
            wtok("vector", tk.get("xattn"))
            cp(b_xattnb[:, :], b_xattn[:, :])
            dv = cnt["dv"]
            b1_prev = (zsf_mm_s, ypre_dv_s, dv)
            wait("gpsimd", "dv", dv)
            tk["xattn"] = dma("gpsimd", st_xattn[s, :, :], b_xattnb[:, :], "gld")
            wait("scalar", "dv", dv)
            for dc in range(8):
                act(b_xsq[:, dc * slab:(dc + 1) * slab], b_xattn[:, dc * slab:(dc + 1) * slab],
                    AF.Square)
            acv = cnt["ac"]
            wait("tensor", "ac", acv)
            bwait("tensor", 0)
            mm(pb[0][0:1, 0:slab], c_onescb[:, 0:1], b_xsq[:, 0:slab], True, False)
            for dc in range(1, 8):
                mm(pb[0][0:1, 0:slab], c_onescb[:, 0:1], b_xsq[:, dc * slab:(dc + 1) * slab],
                   False, dc == 7)
            mv = cnt["mm"]
            wait("vector", "mm", mv)
            ts(p_row[0:1, 0:slab], pb[0][0:1, 0:slab], 1.0, 1e-6, OP.mult, OP.add)
            bset(0, "dv")
            rsqrt_newton(p_row[0:1, slab:2 * slab], p_row[0:1, 0:slab],
                         p_row[0:1, 2 * slab:3 * slab], p_row[0:1, 3 * slab:4 * slab])
            dv = cnt["dv"]
            wait("tensor", "dv", dv)
            bwait("tensor", 1)
            mm(pb[1][:, 0:slab], c_onesr[0:1, :], p_row[0:1, slab:2 * slab], True, True)
            mv = cnt["mm"]
            wait("vector", "mm", mv)
            wtok("vector", tk.get("hin"))
            for dc in range(8):
                tt(b_hin[:, dc * slab:(dc + 1) * slab], b_xattn[:, dc * slab:(dc + 1) * slab],
                   pb[1][:, 0:slab])
                bset(1, "dv") if dc == 7 else None
            dv = cnt["dv"]
            wait("gpsimd", "dv", dv)
            tk["hin"] = dma("gpsimd", st_hin[s, :, :], b_hin[:, :], "gld")

        phase_barrier()
        B2 = Alloc()
        b2_w1h = B2.abf(8 * 2048)
        b2_hin = B2.abf(8 * slab)
        b2_gel = B2.abf(32 * slab)
        b2_w2 = B2.abf(2 * 32 * 128)
        b2_xat = B2.abf(8 * slab)
        b2_att = B2.abf(8 * slab)
        assert B2.bf <= AB_N and B2.f <= AF_N, (B2.bf, B2.f)

        b2_prev = None
        for s in range(nslab):
            if b2_prev is None:
                wait("sync", "dv")
                wait("sync", "mm")
            hin_mm_p = None
            if b2_prev is not None:
                hin_mm_p, xat_dv_p = b2_prev
                wait("sync", "mm", hin_mm_p)
            vh = dma("sync", b2_hin[:, :], st_hin[s, :, :])
            if b2_prev is not None:
                wait("sync", "dv", xat_dv_p)
            vx = dma("sync", b2_xat[:, :], st_xattn[s, :, :])
            for half in range(2):
                if half == 0 and hin_mm_p is not None:
                    wait("sync", "mm", hin_mm_p)
                else:
                    wait("sync", "mm")
                vw = dma("sync", b2_w1h[:, :].rearrange("p (k c) -> p k c", k=8),
                         sc_w1f.rearrange("p (k c) -> p k c", k=8)[:, :, half * 2048:(half + 1) * 2048])
                wtok("tensor", vh)
                wtok("tensor", vw)
                wait("tensor", "dv")
                for mi in range(16):
                    m = half * 16 + mi
                    bank = mi % 8
                    bwait("tensor", bank)
                    for k in range(8):
                        mm(pb[bank][:, 0:slab],
                           b2_w1h[:, k * 2048 + mi * 128:k * 2048 + (mi + 1) * 128],
                           b2_hin[:, k * slab:(k + 1) * slab],
                           start=(k == 0), stop=(k == 7))
                    mv = cnt["mm"]
                    wait("scalar", "mm", mv)
                    act(b2_gel[:, m * slab:(m + 1) * slab], pb[bank][:, 0:slab], AF.Gelu,
                        bias=c_b1[:, m:m + 1])
                    bset(bank, "ac")
            hin_mm_s = cnt["mm"]
            wait("tensor", "ac")
            wtok("scalar", tk.get("attno"))
            w2tok = {}

            def w2load(dc):
                buf_ = b2_w2[:, (dc % 2) * 32 * 128:(dc % 2) * 32 * 128 + 32 * 128]
                wtok("gpsimd", tk.get(("w2b", dc % 2)))
                w2tok[dc] = dma("gpsimd", buf_, sc_w2f[dc], "gld")

            w2load(0)
            for dc in range(8):
                if dc + 1 < 8:
                    w2load(dc + 1)
                buf = b2_w2[:, (dc % 2) * 32 * 128:(dc % 2) * 32 * 128 + 32 * 128]
                wtok("tensor", w2tok[dc])
                bwait("tensor", dc % 4)
                for k in range(32):
                    mm(pb[dc % 4][:, 0:slab], buf[:, k * 128:(k + 1) * 128],
                       b2_gel[:, k * slab:(k + 1) * slab], start=(k == 0), stop=(k == 31))
                mv = cnt["mm"]
                tk[("w2b", dc % 2)] = ("mm", mv)
                wait("scalar", "mm", mv)
                act(b2_att[:, dc * slab:(dc + 1) * slab], pb[dc % 4][:, 0:slab], AF.Identity,
                    bias=c_b2[:, dc:dc + 1])
                bset(dc % 4, "ac")
            acv = cnt["ac"]
            wait("vector", "ac", acv)
            wtok("vector", vx)
            wtok("scalar", tk.get("attno"))
            wait("vector", "ac")
            tt(b2_att[:, :], b2_att[:, :], b2_xat[:, :], OP.add)
            dv = cnt["dv"]
            b2_prev = (hin_mm_s, dv)
            wait("gpsimd", "dv", dv)
            tk["attno"] = dma("gpsimd", st_attno[s, :, :], b2_att[:, :], "gld")

        phase_barrier()
        B3 = Alloc()
        b3_w1q = B3.abf(8 * HMLP)
        b3_xq = B3.abf(8 * slab)
        b3_h = B3.abf(32 * slab)
        b3_xn = B3.af(8 * slab)
        b3_t1 = B3.af(2048)
        b3_ge = B3.af(2048)
        b3_int = B3.af(2048).bitcast(I32)
        assert B3.bf <= AB_N and B3.f <= AF_N, (B3.bf, B3.f)

        v = dma("sync", b3_w1q[:, :], sc_w1q[:, :])
        xn_dv_prev = None
        for s in range(nslab):
            if xn_dv_prev is None:
                wait("sync", "dv")
                wait("sync", "mm")
            else:
                wait("sync", "dv", xn_dv_prev)
            vx = dma("sync", b3_xn[:, :], st_xnfm[s, :, :])
            wtok("vector", vx)
            if s > 0:
                wait("vector", "mm")
            W8 = 8 * slab
            for h0 in range(0, W8, 2048):
                ts(b3_t1[:, 0:2048], b3_xn[:, h0:h0 + 2048], p_rq1[:, 0:1])
                quant_round(b3_xq[:, h0:h0 + 2048], b3_t1[:, 0:2048], 2048, b3_int, b3_ge)
            xn_dv_prev = cnt["dv"]
            dv = cnt["dv"]
            wait("tensor", "dv", dv)
            wtok("tensor", v)
            wtok("scalar", tk.get("h"))
            for m in range(32):
                bank = m % 8
                bwait("tensor", bank)
                for k in range(8):
                    mm(pb[bank][:, 0:slab],
                       b3_w1q[:, k * HMLP + m * 128:k * HMLP + (m + 1) * 128],
                       b3_xq[:, k * slab:(k + 1) * slab], start=(k == 0), stop=(k == 7))
                mv = cnt["mm"]
                wait("scalar", "mm", mv)
                act(b3_h[:, m * slab:(m + 1) * slab], pb[bank][:, 0:slab], AF.Gelu,
                    scale=p_fs1[:, 0:1], bias=p_b1s[:, m:m + 1])
                bset(bank, "ac")
            acv = cnt["ac"]
            wait("vector", "ac", acv)
            red(p_rr[:, 11:12], b3_h[:, 0:32 * slab])
            tt(p_maxh[:, :], p_maxh[:, :], p_rr[:, 11:12], OP.max)
            wait("gpsimd", "ac", acv)
            tk["h"] = dma("gpsimd", st_h[s, :, :], b3_h[:, :], "gld")

        phase_barrier()
        wait("tensor", "dv")
        bwait("tensor", 0)
        tr(pb[0][0:1, 0:128], p_maxh[:, :], c_eye[:, :])
        mv = cnt["mm"]
        wait("vector", "mm", mv)
        red(p_sc[0:1, 14:15], pb[0][0:1, 0:128], OP.max, absv=False)
        bset(0, "dv")
        dv = cnt["dv"]
        wait("gpsimd", "dv", dv)
        dma("gpsimd", cc2_in[0:1, 0:1], p_sc[0:1, 14:15], "gld")
        wq("gpsimd", "gld")
        op("gpsimd", lambda E: E.collective_compute(
            "AllGather", OP.bypass, replica_groups=[list(range(8))],
            ins=[cc2_in.opt()], outs=[cc2_out.opt()]), "cc", 1)
        wait("gpsimd", "cc", cnt["cc"])
        t3_ = dma("gpsimd", p_w8r[0:1, 0:8], cc2_out[:, 0:1], "gld")
        wtok("vector", t3_)
        red(p_sc[0:1, 7:8], p_w8r[0:1, 0:8], OP.max, absv=False)
        ts(p_sc[0:1, 8:9], p_sc[0:1, 7:8], 1e-6, 1.0 / 127.0, OP.max, OP.mult)
        op("vector", lambda E: E.reciprocal(p_sc[0:1, 9:10], p_sc[0:1, 8:9]), "dv")
        tt(p_sc[0:1, 11:12], p_sc[0:1, 2:3], p_sc[0:1, 8:9])
        bcast_col(p_rq2[:, 0:1], p_sc[0:1, 9:10])
        bcast_col(p_fs2[:, 0:1], p_sc[0:1, 11:12])
        ts(p_fb2s[:, :], c_fb2[:, :], p_fs2[:, 0:1])

        phase_barrier()
        C = Alloc()
        c_h = C.abf(32 * slab)
        c_hq = C.abf(32 * slab)
        c_w2q = C.abf(2 * 8 * 512)
        c_att = C.abf(8 * slab)
        c_tot = C.af(8 * slab)
        c_t1 = C.af(2048)
        c_ge2 = C.af(2048)
        c_int2 = c_tot.bitcast(I32)[:, 0:2048]
        c_xpx = c_t1[:, 0:D]
        c_out = C.af(D)
        assert C.bf <= AB_N and C.f <= AF_N, (C.bf, C.f)

        q_dv_prev = None
        att_dv_prev = None
        cmm_p = [None]
        for s in range(nslab):
            if q_dv_prev is None:
                wait("sync", "dv")
                wait("sync", "mm")
            else:
                wait("sync", "dv", q_dv_prev)
            vh = dma("sync", c_h[:, :], st_h[s, :, :])
            if att_dv_prev is not None:
                wait("sync", "dv", att_dv_prev)
            va = dma("sync", c_att[:, :], st_attno[s, :, :])
            wtok("vector", vh)
            if s > 0:
                wait("vector", "mm")
            nstrip = (32 * slab) // 2048
            ksper = 32 // nstrip
            qdv = {}
            wtoks = {}

            def qstrip(st):
                sl = slice(st * 2048, (st + 1) * 2048)
                ts(c_t1[:, 0:2048], c_h[:, sl], p_rq2[:, 0:1])
                quant_round(c_hq[:, sl], c_t1[:, 0:2048], 2048, c_int2, c_ge2)
                qdv[st] = cnt["dv"]

            def wload(st):
                par = (st % 2) * 8 * 512
                if st < 2 and cmm_p[0] is not None:
                    wait("sync", "mm", cmm_p[0])
                else:
                    wait("sync", "mm")
                wtoks[st] = dma("sync",
                    c_w2q[:, par:par + 8 * ksper * 128].rearrange("p (d c) -> p d c", d=8),
                    sc_w2q[:, :, st * ksper * 128:(st + 1) * ksper * 128]
                    .rearrange("d p c -> p d c"))

            wload(0)
            qstrip(0)
            for st in range(nstrip):
                if st + 1 < nstrip:
                    wload(st + 1)
                    qstrip(st + 1)
                par = (st % 2) * 8 * 512
                wtok("tensor", wtoks[st])
                wait("tensor", "dv", qdv[st])
                for dc in range(8):
                    if st == 0:
                        bwait("tensor", dc)
                    for kk in range(ksper):
                        k = st * ksper + kk
                        mm(pb[dc][:, 0:slab],
                           c_w2q[:, par + dc * ksper * 128 + kk * 128:par + dc * ksper * 128 + (kk + 1) * 128],
                           c_hq[:, k * slab:(k + 1) * slab],
                           start=(st == 0 and kk == 0), stop=(st == nstrip - 1 and kk == ksper - 1))
            mv = cnt["mm"]
            cmm_p[0] = mv
            wait("scalar", "mm", mv)
            for dc in range(8):
                act(c_tot[:, dc * slab:(dc + 1) * slab], pb[dc][:, 0:slab], AF.Identity,
                    scale=p_fs2[:, 0:1], bias=p_fb2s[:, dc:dc + 1])
                bset(dc, "ac")
            acv = cnt["ac"]
            wait("vector", "ac", acv)
            wtok("vector", va)
            tt(c_tot[:, :], c_tot[:, :], c_att[:, :], OP.add)
            q_dv_prev = qdv[nstrip - 1]
            att_dv_prev = cnt["dv"]
            for i in range(slab // 128):
                wait("sync", "dv")
                vx = dma("sync", c_xpx[:, 0:D],
                         xin[128 + s * slab + i * 128:128 + s * slab + (i + 1) * 128, :])
                for w0 in range(0, 8, 4):
                    wait("tensor", "dv")
                    for j in range(4):
                        dck = w0 + j
                        tr(pb[4 + j][:, 0:128], c_tot[:, dck * slab + i * 128:dck * slab + (i + 1) * 128],
                           c_eye[:, :])
                    mv = cnt["mm"]
                    wait("vector", "mm", mv)
                    wtok("vector", vx)
                    if w0 == 0:
                        wtok("vector", tk.get("out"))
                    for j in range(4):
                        dck = w0 + j
                        tt(c_out[:, dck * 128:(dck + 1) * 128], pb[4 + j][:, 0:128],
                           c_xpx[:, dck * 128:(dck + 1) * 128], OP.add)
                dv = cnt["dv"]
                wait("gpsimd", "dv", dv)
                tk["out"] = dma("gpsimd", out[s * slab + i * 128:s * slab + (i + 1) * 128, :], c_out[:, 0:D], "ost")

        if self.debug_outs:
            stmap = {"ypre": st_ypre, "zs": st_zs, "xnfm": st_xnfm, "xattn": st_xattn,
                     "hin": st_hin, "attno": st_attno, "h": st_h, "cc1o": cc1_out,
                     "cc2o": cc2_out, "w1q": sc_w1q, "w2q": sc_w2q,
                     "psc": p_sc, "b1s": p_b1s, "fs1": p_fs1, "fs2": p_fs2,
                     "rq1": p_rq1, "rq2": p_rq2}
            wq("gpsimd", "sld")
            wq("gpsimd", "gld")
            wait("gpsimd", "dv")
            wait("gpsimd", "ac")
            wait("gpsimd", "mm")
            for name in self.debug_outs:
                src = stmap[name]
                d = nc.dram_tensor("dbg_" + name, list(src.shape), src.dtype,
                                   kind="ExternalOutput").ap()
                dma("gpsimd", d, src, "gld")
        wq("gpsimd", "ost")
        wq("gpsimd", "gld")

        with nc.Block() as block:
            @block.sync
            def _(E):
                for f in self.prog["sync"]:
                    f(E)

            @block.scalar
            def _(E):
                for f in self.prog["scalar"]:
                    f(E)

            @block.vector
            def _(E):
                for f in self.prog["vector"]:
                    f(E)

            @block.tensor
            def _(E):
                for f in self.prog["tensor"]:
                    f(E)

            @block.gpsimd
            def _(E):
                for f in self.prog["gpsimd"]:
                    f(E)
        return nc


def make_inputs(full, T, core):
    b, q = core // 4, core % 4
    L = full["L0_attn_x"].shape[1]
    Tq = L // 4
    assert Tq == T
    x = np.asarray(full["L0_attn_x"][b], np.float32)
    sh = np.zeros((128 + T, D), np.float32)
    t0 = q * Tq
    sh[128:, :] = x[t0:t0 + Tq]
    if q > 0:
        sh[0:128, :] = x[t0 - 128:t0]
    ones = np.ones((128, 128), np.float32)
    g = core // 4
    p = core % 4
    gm = np.zeros((8, 1), np.float32)
    um = np.zeros((8, 8), np.float32)
    for j in range(8):
        if j // 4 == g and j % 4 < p:
            gm[j, 0] = 1.0
            for r in range(8):
                if r // 4 == g and (j % 4) < (r % 4) < p:
                    um[j, r] = 1.0
    dt = lambda a: np.ascontiguousarray(np.asarray(a, np.float32))
    return {
        "xin": sh,
        "win_t": dt(full["L0_attn_in_proj_w"]).T.copy(),
        "wout_t": dt(full["L0_attn_out_proj_w"]).T.copy(),
        "w1_t": dt(full["L0_attn_mlp_w1"]).T.copy(),
        "w2_t": dt(full["L0_attn_mlp_w2"]).T.copy(),
        "fw1_t": dt(full["L0_ffn_w1"]).T.copy(),
        "fw2_t": dt(full["L0_ffn_w2"]).T.copy(),
        "convw": dt(full["L0_attn_conv_w"]).reshape(NCC, 128, 4).transpose(1, 0, 2).reshape(128, NCC * 4).copy(),
        "convb": dt(full["L0_attn_conv_b"]).reshape(NCC, 128).T.copy(),
        "norm1w": dt(full["L0_attn_norm1_w"]).reshape(8, 128).T.copy(),
        "norm2w": dt(full["L0_attn_norm2_w"]).reshape(8, 128).T.copy(),
        "mnormw": dt(full["L0_attn_mnorm_w"]).reshape(16, 128).T.copy(),
        "dtb": dt(full["L0_attn_dt_bias"]).reshape(-1, 1),
        "alog": dt(full["L0_attn_A_log"]).reshape(-1, 1),
        "alog_row": dt(full["L0_attn_A_log"]).reshape(1, -1),
        "dtile": np.tile(np.repeat(dt(full["L0_attn_D"]), HD)[None, :], (128, 1)).copy(),
        "b1": dt(full["L0_attn_mlp_b1"]).reshape(32, 128).T.copy(),
        "b2": dt(full["L0_attn_mlp_b2"]).reshape(8, 128).T.copy(),
        "fb1": dt(full["L0_ffn_b1"]).reshape(32, 128).T.copy(),
        "fb2": dt(full["L0_ffn_b2"]).reshape(8, 128).T.copy(),
        "eye_f": np.eye(128, dtype=np.float32),
        "triu_f": np.triu(ones).astype(np.float32),
        "ptl_f": np.tril(ones, -1).astype(np.float32),
        "ones_row": np.ones((1, 128), np.float32),
        "gmask": gm,
        "umat_t": um.T.copy(),
    }


def kernel(**inputs):
    T = inputs["L0_attn_x"].shape[1] // 4
    slab = 512 if T % 512 == 0 else T
    b = Builder(T, slab)
    nc = b.build()
    in_maps = [make_inputs(inputs, T, core) for core in range(8)]
    res = run_bass_kernel_spmd(nc, in_maps, core_ids=list(range(8)))
    B, L, _ = inputs["L0_attn_x"].shape
    outp = np.empty((B, L, D), np.float32)
    for core in range(8):
        bq, q = core // 4, core % 4
        outp[bq, q * T:(q + 1) * T, :] = res.results[core]["out"]
    return outp
